# version 17
# baseline (speedup 1.0000x reference)
import numpy as np
from contextlib import ExitStack
import concourse.bass as bass
import concourse.mybir as mybir

F32 = mybir.dt.float32
BF16 = mybir.dt.bfloat16
F16 = mybir.dt.float16
I32 = mybir.dt.int32
AF = mybir.ActivationFunctionType
ALU = mybir.AluOpType
AX = mybir.AxisListType

P = 128
DM = 1024
DI = 2816
NH = 16
DH = 64
NSPAN = 6
SPANS = [(i * 512, min(512, DI - i * 512)) for i in range(NSPAN)]
NKC = DI // P
GROUP = 2
EPS_LN = 1e-5
EPS_ATTN = 1e-6


class Sched:
    ENGS = ("sync", "gpsimd", "scalar", "vector", "tensor")
    ESEM = {"vector": "sDVE", "scalar": "sACT", "gpsimd": "sPOOL",
            "tensor": "sPE", "sync": "sSP"}

    def __init__(self):
        self.steps = {e: [] for e in self.ENGS}
        self.counts = {}
        self.writers = {}
        self.readers = {}

    def step(self, eng, emit, reads=(), writes=(), waits=(), sem="auto", amt=1):
        if sem == "auto":
            sem = self.ESEM[eng]
        wl = set(w for w in waits if w is not None)
        for r in reads:
            t = self.writers.get(r)
            if t is not None:
                wl.add(t)
        for w in writes:
            t = self.writers.get(w)
            if t is not None:
                wl.add(t)
            for t2 in self.readers.get(w, ()):
                wl.add(t2)
        c = self.counts.get(sem, 0) + amt
        self.counts[sem] = c
        tok = (sem, c)
        self.steps[eng].append((emit, sorted(wl), sem, amt))
        for r in reads:
            self.readers.setdefault(r, []).append(tok)
        for w in writes:
            self.writers[w] = tok
            self.readers[w] = []
        return tok

    def barrier(self):
        toks = []
        for e in self.ENGS:
            sem = self.ESEM[e]
            toks.append((sem, self.counts.get(sem, 0)))
        for e in self.ENGS:
            self.step(e, lambda eng: eng.nop(), waits=[t for t in toks])

    def emit_all(self, block, sems):
        for eng_name in self.ENGS:
            steps = self.steps[eng_name]

            def mk(eng_name=eng_name, steps=steps):
                def body(eng):
                    waited = {}
                    own = self.ESEM[eng_name]
                    for (emit, wl, sem, amt) in steps:
                        for (s_, c_) in wl:
                            if s_ == own and eng_name == "tensor":
                                continue
                            if waited.get(s_, -1) < c_:
                                eng.wait_ge(sems[s_], c_)
                                waited[s_] = c_
                        inst = emit(eng)
                        if inst is not None:
                            inst.then_inc(sems[sem], amt)
                return body
            getattr(block, eng_name)(mk())




def build(nc, T, n_cores_collective, replica_groups, dbg=False, stop_after="Z",
          c_upto=9):
    TOK = T * P
    stack = ExitStack()
    en = stack.enter_context
    en(nc.allow_low_precision(
        "f16 tiles only carry fp4-quantization decisions / staircase sums; "
        "validated numerically with 4x margin vs the tolerance"))

    x_in = nc.dram_tensor("x_in", [TOK, DM], F32, kind="ExternalInput").ap()
    wqkv_in = nc.dram_tensor("wqkv", [DM, 3 * DM], F32, kind="ExternalInput").ap()
    wout_in = nc.dram_tensor("wout", [DM, DM], F32, kind="ExternalInput").ap()
    bout_in = nc.dram_tensor("bout", [1, DM], F32, kind="ExternalInput").ap()
    wg_in = nc.dram_tensor("wg", [DI, DM], F32, kind="ExternalInput").ap()
    wu_in = nc.dram_tensor("wu", [DI, DM], F32, kind="ExternalInput").ap()
    wd_in = nc.dram_tensor("wd", [DM, DI], F32, kind="ExternalInput").ap()
    y_out = nc.dram_tensor("y_out", [TOK, DM], F32, kind="ExternalOutput").ap()

    wgu_spill = nc.dram_tensor("wgu_spill", [NSPAN, P, 2, 8, 512], BF16).ap()
    wd_spill = nc.dram_tensor("wd_spill", [P, NKC, DM], BF16).ap()
    kv_src = nc.dram_tensor("kv_src", [64, 1040], F32).ap()
    kv_dst = nc.dram_tensor("kv_dst", [64, 1040], F32).ap()

    s = Sched()
    S = s.step

    x_res = en(nc.sbuf_tensor("x_res", [P, T, DM], BF16))
    ident = en(nc.sbuf_tensor("ident", [P, P], BF16))

    ones_col = en(nc.sbuf_tensor("ones_col", [1, P], BF16))


    nb = 2
    nq = 3
    WQ = GROUP * 512
    wA = [en(nc.sbuf_tensor(f"wA{i}", [P, WQ], F16)) for i in range(nq)]
    wG = [en(nc.sbuf_tensor(f"wG{i}", [P, WQ], F16)) for i in range(nq)]
    wT1 = [en(nc.sbuf_tensor(f"wT1{i}", [P, WQ], F16)) for i in range(nq)]
    wSS = [en(nc.sbuf_tensor(f"wSS{i}", [P, WQ], F16)) for i in range(nq)]
    h_sp = [en(nc.sbuf_tensor(f"h_sp{i}", [P, WQ], F16)) for i in range(nq)]
    wU = h_sp
    g_gm = [en(nc.sbuf_tensor(f"g_gm{i}", [P, 64], F16)) for i in range(nq)]
    g_r = [en(nc.sbuf_tensor(f"g_r{i}", [P, 64], F16)) for i in range(nq)]
    g_sh = [en(nc.sbuf_tensor(f"g_sh{i}", [P, 64], F16)) for i in range(nq)]
    qx = [en(nc.sbuf_tensor(f"qx{i}", [P, DM], BF16)) for i in range(nb)]
    xqT = [en(nc.sbuf_tensor(f"xqT{i}", [P, 8, P], BF16)) for i in range(2 * GROUP)]

    hqT = [en(nc.sbuf_tensor(f"hqT{i}", [P, NKC, P], BF16)) for i in range(GROUP)]
    qh_all = en(nc.sbuf_tensor("qh_all", [P, GROUP, DI], BF16))
    lnsum = [en(nc.sbuf_tensor(f"lnsum{i}", [P, 1], F32)) for i in range(nb)]
    lnss = [en(nc.sbuf_tensor(f"lnss{i}", [P, 1], F32)) for i in range(nb)]
    lnm = [en(nc.sbuf_tensor(f"lnm{i}", [P, 1], F32)) for i in range(nb)]
    lnv = [en(nc.sbuf_tensor(f"lnv{i}", [P, 1], F32)) for i in range(nb)]
    rstd = [en(nc.sbuf_tensor(f"rstd{i}", [P, 1], F32)) for i in range(nb)]
    nbias = [en(nc.sbuf_tensor(f"nbias{i}", [P, 1], F32)) for i in range(nb)]



    pb = [en(nc.psum_tensor(f"pb{i}", [P, 512], F32)) for i in range(8)]

    sem_names = ["sDVE", "sACT", "sPOOL", "sPE", "sSP",
                 "dXIN", "dW0", "dW1", "dMISC", "dSPILL", "dOUT", "dKV", "cKV",
                 "dBOUT", "dWKV", "dWQ", "dWOUT", "dKVL", "dWDT", "dSPILL1", "dMISC1", "dWU0", "dWU1"]
    sems = {n: en(nc.semaphore(n)) for n in sem_names}

    block = en(nc.Block())

    def view16(ap, w):
        return ap.rearrange("p (g w) -> p g w", w=16)

    def bc16(small_ap, g16):
        return small_ap.unsqueeze(2).broadcast_to([P, g16, 16])

    def quant_chain(par, width, src_emit_abs, src_emit_sgn, src_reads,
                    q_dst_ap, q_write_res, extra_waits=(), B=None):
        g16 = width // 16
        if B is None:
            A_, G_, U_, T1, SS = wA[par], wG[par], wU[par], wT1[par], wSS[par]
            T2 = A_
            GM, R_, SH = g_gm[par], g_r[par], g_sh[par]
            ra = rt2 = f"wA{par}"
            rg, ru, rt1, rss = f"wG{par}", f"h_sp{par}", f"wT1{par}", f"wSS{par}"
            rgm, rr, rsh = f"g_gm{par}", f"g_r{par}", f"g_sh{par}"
        else:
            bufs, kp = B
            A_, G_, U_, T1, T2, SS, GM, SC_, R_, SH = (b[par] for b in bufs)
            ra, rg, ru, rt1, rt2, rss = (f"{kp}a{par}", f"{kp}g{par}",
                                         f"{kp}u{par}", f"{kp}t1{par}",
                                         f"{kp}t2{par}", f"{kp}ss{par}")
            rgm, rr, rsh = f"{kp}gm{par}", f"{kp}r{par}", f"{kp}sh{par}"

        S("scalar", lambda e: src_emit_abs(e, A_[:, :width]),
          reads=src_reads, writes=[ra], waits=extra_waits)
        S("scalar", lambda e: src_emit_sgn(e, G_[:, :width]),
          reads=src_reads, writes=[rg], waits=extra_waits)
        S("vector", lambda e: e.tensor_reduce(
            GM[:, :g16], view16(A_[:, :width], width), AX.X, ALU.max),
          reads=[ra], writes=[rgm])
        S("vector", lambda e: e.tensor_scalar(
            SH[:, :g16], GM[:, :g16], 1.0 / 12.0, 5e-13, ALU.mult, ALU.max),
          reads=[rgm], writes=[rsh])
        S("vector", lambda e: e.reciprocal(R_[:, :g16], GM[:, :g16]),
          reads=[rgm], writes=[rr])
        S("vector", lambda e: e.tensor_scalar(
            R_[:, :g16], R_[:, :g16], 6.0, 1e12, ALU.mult, ALU.min),
          reads=[rr], writes=[rr])
        S("gpsimd", lambda e: e.tensor_tensor(
            view16(U_[:, :width], width), view16(A_[:, :width], width),
            bc16(R_[:, :g16], g16), ALU.mult),
          reads=[ra, rr], writes=[ru])
        S("gpsimd", lambda e: e.tensor_tensor(
            view16(SS[:, :width], width), view16(G_[:, :width], width),
            bc16(SH[:, :g16], g16), ALU.mult),
          reads=[rg, rsh], writes=[rss])
        S("vector", lambda e: e.tensor_scalar(
            T1[:, :width], U_[:, :width], 5.0, 2.0, ALU.is_gt, ALU.mult),
          reads=[ru], writes=[rt1])
        S("vector", lambda e: e.scalar_tensor_tensor(
            T2[:, :width], U_[:, :width], 3.5, T1[:, :width], ALU.is_gt, ALU.add),
          reads=[ru, rt1], writes=[rt2])
        S("vector", lambda e: e.scalar_tensor_tensor(
            T1[:, :width], U_[:, :width], 2.5, T2[:, :width], ALU.is_gt, ALU.add),
          reads=[ru, rt2], writes=[rt1])
        S("vector", lambda e: e.tensor_scalar_mul(
            T2[:, :width], T1[:, :width], 2.0),
          reads=[rt1], writes=[rt2])
        S("vector", lambda e: e.scalar_tensor_tensor(
            T1[:, :width], U_[:, :width], 1.75, T2[:, :width], ALU.is_gt, ALU.add),
          reads=[ru, rt2], writes=[rt1])
        S("vector", lambda e: e.scalar_tensor_tensor(
            T2[:, :width], U_[:, :width], 1.25, T1[:, :width], ALU.is_gt, ALU.add),
          reads=[ru, rt1], writes=[rt2])
        S("vector", lambda e: e.scalar_tensor_tensor(
            T1[:, :width], U_[:, :width], 0.75, T2[:, :width], ALU.is_gt, ALU.add),
          reads=[ru, rt2], writes=[rt1])
        S("vector", lambda e: e.scalar_tensor_tensor(
            T2[:, :width], U_[:, :width], 0.25, T1[:, :width], ALU.is_gt, ALU.add),
          reads=[ru, rt1], writes=[rt2])
        tq = S("gpsimd", lambda e: e.tensor_mul(
            q_dst_ap, T2[:, :width], SS[:, :width]),
          reads=[rt2, rss], writes=[q_write_res])
        return tq

    def transposes(src_fn, nchunks, dst_fn, src_reads, dst_writes, bank_idx):
        i = 0
        while i < nchunks:
            n = min(8, nchunks - i)
            bank = pb[bank_idx].ap().bitcast(BF16)
            for j in range(n):
                S("tensor", lambda e, jj=i + j, oj=j, b=bank: e.transpose(
                    b[:, oj * P:(oj + 1) * P], src_fn(jj), ident[:]),
                  reads=src_reads + ["ident"], writes=[f"pb{bank_idx}"])
            S("scalar", lambda e, ii=i, n_=n, b=bank: e.copy(
                dst_fn(ii, n_), b[:, :n_ * P]),
              reads=[f"pb{bank_idx}"], writes=dst_writes)
            i += n

    def ln_stats(par, x_ap, x_res_key):
        S("scalar", lambda e: e.activation(
            qx[par][:], x_ap, AF.Identity, accum_out=lnsum[par][:]),
          reads=[x_res_key],
          writes=[f"lnsum{par}", f"qx{par}p0", f"qx{par}p1"])
        S("scalar", lambda e: e.activation(
            qx[par][:], x_ap, AF.Square, accum_out=lnss[par][:]),
          reads=[x_res_key],
          writes=[f"lnss{par}", f"qx{par}p0", f"qx{par}p1"])
        S("vector", lambda e: e.tensor_scalar_mul(
            lnm[par][:], lnsum[par][:], 1.0 / DM),
          reads=[f"lnsum{par}"], writes=[f"lnm{par}"])
        S("vector", lambda e: e.tensor_scalar_mul(
            lnv[par][:], lnss[par][:], 1.0 / DM),
          reads=[f"lnss{par}"], writes=[f"lnv{par}"])
        S("vector", lambda e: e.scalar_tensor_tensor(
            lnv[par][:], lnm[par][:], lnm[par][:, 0:1], lnv[par][:],
            ALU.mult, ALU.subtract),
          reads=[f"lnm{par}", f"lnv{par}"], writes=[f"lnv{par}"])
        S("scalar", lambda e: e.activation(
            rstd[par][:], lnv[par][:], AF.Ln, bias=np.float32(EPS_LN), scale=np.float32(-1.0)),
          reads=[f"lnv{par}"], writes=[f"rstd{par}"])
        S("scalar", lambda e: e.activation(
            rstd[par][:], rstd[par][:], AF.Exp, scale=np.float32(-0.5)),
          reads=[f"rstd{par}"], writes=[f"rstd{par}"])
        S("vector", lambda e: e.scalar_tensor_tensor(
            nbias[par][:], lnm[par][:], -1.0, rstd[par][:], ALU.mult, ALU.mult),
          reads=[f"lnm{par}", f"rstd{par}"], writes=[f"nbias{par}"])

    sustack = ExitStack()
    identi = sustack.enter_context(nc.sbuf_tensor("identi", [P, P], I32))
    wstage = [sustack.enter_context(nc.sbuf_tensor(f"wstage{i}", [P, DM], F32))
              for i in range(2)]

    x_view = x_in.rearrange("(t p) d -> p t d", p=P)
    nxd = 4 if T >= 4 else 1
    step_t = (T + nxd - 1) // nxd
    for i in range(0, T, step_t):
        n = min(step_t, T - i)
        S("gpsimd", lambda e, i_=i, n_=n: e.dma_start(
            out=x_res[:, i_:i_ + n_, :], in_=x_view[:, i_:i_ + n_, :]),
          writes=[f"x:{t}" for t in range(i, i + n)], sem="dXIN", amt=16)
    tok_x_all = ("dXIN", s.counts["dXIN"])
    S("sync", lambda e: e.nop(), waits=[tok_x_all],
      writes=[f"x:{t}" for t in range(T)])

    S("gpsimd", lambda e: e.iota(identi[:], pattern=[[1, P]], base=0,
                                 channel_multiplier=-1), writes=["identi"])
    S("gpsimd", lambda e: e.tensor_scalar(
        ident[:], identi[:], 0.0, None, ALU.is_equal),
      reads=["identi"], writes=["ident"])
    S("vector", lambda e: e.memset(ones_col[:], 1.0), writes=["ones_col"])

    spill_tok = {}
    wgv = wg_in.rearrange("(c p) d -> c p d", p=P)
    wuv = wu_in.rearrange("(c p) d -> c p d", p=P)
    for c in range(NKC):
        for mi, wv in ((0, wgv[c]), (1, wuv[c])):
            par = (2 * c + mi) % nq
            wpar = (2 * c + mi) % 2
            ws = wstage[wpar]
            wsk = f"wstage{wpar}"
            S("sync", lambda e, wv_=wv, w_=ws: e.dma_start(out=w_[:], in_=wv_),
              writes=[wsk], sem=("dMISC" if wpar == 0 else "dMISC1"), amt=16)
            qpar = wpar
            quant_chain(
                par, DM,
                lambda e, d, w_=ws: e.activation(d, w_[:], AF.Abs),
                lambda e, d, w_=ws: e.activation(d, w_[:], AF.Sign),
                [wsk],
                qx[qpar][:], f"qx{qpar}p0")
            transposes(
                lambda i, q_=qx[qpar]: q_[:, i * P:(i + 1) * P], 8,
                lambda i0_, n, b=xqT[qpar]: b[:, i0_:i0_ + n, :].rearrange(
                    "p c f -> p (c f)"),
                [f"qx{qpar}p0"], [f"xqT{qpar}"], 6 + qpar)
            sp_, off_ = (c * P) // 512, (c * P) % 512
            spill_tok[(mi, c)] = S(
                "gpsimd", lambda e, s_=sp_, o_=off_, b=xqT[qpar], si=mi:
                e.dma_start(out=wgu_spill[s_, :, si, :, o_:o_ + P], in_=b[:]),
                reads=[f"xqT{qpar}"],
                sem=("dSPILL" if qpar == 0 else "dSPILL1"), amt=16)

    with ExitStack() as wstack:
        wen = wstack.enter_context
        w32 = tuple(
            [wen(nc.sbuf_tensor(f"w32{nm}{i}", [P, w_], F32)) for i in range(nb)]
            for nm, w_ in (("a", 1024), ("g", 1024), ("u", 1024), ("t1", 1024),
                           ("t2", 1024), ("ss", 1024), ("gm", 64), ("sc", 64),
                           ("r", 64), ("sh", 64)))
        B32 = (w32, "w32")
        wdv = wd_in.rearrange("(c p) d -> c p d", p=P)
        for c in range(8):
            for piece in range(3):
                poff = piece * 1024
                plen = min(1024, DI - poff)
                par = (3 * c + piece) % 2
                ws = wstage[par]
                wsk = f"wstage{par}"
                S("sync", lambda e, wv_=wdv[c], o=poff, L=plen, w_=ws:
                  e.dma_start(out=w_[:, 0:L], in_=wv_[:, o:o + L]),
                  writes=[wsk], sem=("dMISC" if par == 0 else "dMISC1"), amt=16)
                quant_chain(
                    par, plen,
                    lambda e, d, L=plen, w_=ws: e.activation(
                        d, w_[:, 0:L], AF.Abs),
                    lambda e, d, L=plen, w_=ws: e.activation(
                        d, w_[:, 0:L], AF.Sign),
                    [wsk],
                    qx[par][:, :plen], f"qx{par}p0", B=B32)
                nch = plen // P
                transposes(
                    lambda i, q_=qx[par]: q_[:, i * P:(i + 1) * P], nch,
                    lambda i0_, n, b=xqT[par]: b[:, i0_:i0_ + n, :].rearrange(
                        "p c f -> p (c f)"),
                    [f"qx{par}p0"], [f"xqT{par}"], 6 + par)
                spill_tok[(2, 3 * c + piece)] = S(
                    "gpsimd", lambda e, b=xqT[par], k0=poff // P, n=nch, cc=c:
                    e.dma_start(out=wd_spill[:, k0:k0 + n, cc * P:(cc + 1) * P],
                                in_=b[:, 0:n, :]),
                    reads=[f"xqT{par}"],
                    sem=("dSPILL" if par == 0 else "dSPILL1"), amt=16)

    tok_spill_all = ("dSPILL", s.counts["dSPILL"])
    tok_spill_all2 = ("dSPILL1", s.counts["dSPILL1"])
    s.barrier()
    sustack.close()

    def ffn_phase(phase, wdT, wgu, out_dma=False):
        wdT_key = f"{phase}:wdT"
        S("sync", lambda e: e.dma_start(out=wdT[:], in_=wd_spill),
          waits=[spill_tok[(2, i_)] for i_ in range(24)], writes=[wdT_key],
          sem="dWDT", amt=16)

        ngroups = (T + GROUP - 1) // GROUP
        all_tiles = [list(range(g * GROUP, min(T, (g + 1) * GROUP)))
                     for g in range(ngroups)]

        def emit_xquant(g):
            sb = (g % 2) * GROUP
            for t in all_tiles[g]:
                par = t % 2
                slot = sb + t % GROUP
                xk = f"x:{t}"
                x_ap = x_res[:, t, :]
                ln_stats(par, x_ap, xk)
                quant_chain(
                    t % nq, DM,
                    lambda e, d, xa=x_ap, p_=par: e.activation(
                        d, xa, AF.Abs,
                        bias=nbias[p_][:, 0:1], scale=rstd[p_][:, 0:1]),
                    lambda e, d, xa=x_ap, p_=par: e.activation(
                        d, xa, AF.Sign,
                        bias=nbias[p_][:, 0:1], scale=rstd[p_][:, 0:1]),
                    [xk, f"rstd{par}", f"nbias{par}"],
                    qx[par][:], f"qx{par}p0")
                transposes(
                    lambda i, q_=qx[par]: q_[:, i * P:(i + 1) * P], 8,
                    lambda i0, n, b=xqT[slot]: b[:, i0:i0 + n, :].rearrange(
                        "p c f -> p (c f)"),
                    [f"qx{par}p0"], [f"xqT{slot}"], 6 + par)

        emit_xquant(0)
        for g in range(ngroups):
            tiles = all_tiles[g]
            sb = (g % 2) * GROUP
            if g + 1 < ngroups:
                emit_xquant(g + 1)

            for sp in range(NSPAN):
                soff, slen = SPANS[sp]
                buf = sp % 2
                wk = f"{phase}:wgu{buf}"
                dsem = "dW0" if buf == 0 else "dW1"
                wkg, wku = wk + "g", wk + "u"
                dsem_u = "dWU0" if buf == 0 else "dWU1"
                cw = range(sp * 4, min(NKC, sp * 4 + 4))
                S("sync", lambda e, sp_=sp, b=wgu[buf], L=slen: e.dma_start(
                    out=b[:, 0, :, 0:L], in_=wgu_spill[sp_][:, 0, :, 0:L]),
                  waits=[spill_tok[(0, c_)] for c_ in cw], writes=[wkg],
                  sem=dsem, amt=16)
                S("sync", lambda e, sp_=sp, b=wgu[buf], L=slen: e.dma_start(
                    out=b[:, 1, :, 0:L], in_=wgu_spill[sp_][:, 1, :, 0:L]),
                  waits=[spill_tok[(1, c_)] for c_ in cw], writes=[wku],
                  sem=dsem_u, amt=16)
                par2 = sp % nq
                ntl = len(tiles)
                for t in tiles:
                    gsl = t % GROUP
                    it = sp * GROUP + gsl
                    pp2 = it % 2
                    bg, bu = (0, 1) if pp2 == 0 else (2, 3)
                    slot = sb + gsl
                    for k in range(8):
                        S("tensor", lambda e, k_=k, b=wgu[buf], pg=pb[bg],
                          xt=xqT[slot], L=slen: e.matmul(
                              pg[:, :L], xt[:, k_, :], b[:, 0, k_, 0:L],
                              start=(k_ == 0), stop=(k_ == 7)),
                          reads=[f"xqT{slot}", wkg], writes=[f"pb{bg}"])
                    for k in range(8):
                        S("tensor", lambda e, k_=k, b=wgu[buf], pu=pb[bu],
                          xt=xqT[slot], L=slen: e.matmul(
                              pu[:, :L], xt[:, k_, :], b[:, 1, k_, 0:L],
                              start=(k_ == 0), stop=(k_ == 7)),
                          reads=[f"xqT{slot}", wku], writes=[f"pb{bu}"])
                    S("scalar", lambda e, pg=pb[bg], L=slen, p2=par2, g_=gsl:
                      e.activation(
                          wA[p2][:, g_ * L:(g_ + 1) * L], pg[:, :L], AF.Silu),
                      reads=[f"pb{bg}"], writes=[f"wA{par2}"])
                    S("scalar", lambda e, pu=pb[bu], L=slen, p2=par2, g_=gsl:
                      e.copy(wT1[p2][:, g_ * L:(g_ + 1) * L], pu[:, :L]),
                      reads=[f"pb{bu}"], writes=[f"wT1{par2}"])
                    S("gpsimd", lambda e, L=slen, p2=par2, g_=gsl:
                      e.tensor_mul(
                          h_sp[p2][:, g_ * L:(g_ + 1) * L],
                          wA[p2][:, g_ * L:(g_ + 1) * L],
                          wT1[p2][:, g_ * L:(g_ + 1) * L]),
                      reads=[f"wA{par2}", f"wT1{par2}"], writes=[f"h_sp{par2}"])
                wqw = ntl * slen
                quant_chain(
                    par2, wqw,
                    lambda e, d, p2=par2, W_=wqw: e.activation(
                        d, h_sp[p2][:, :W_], AF.Abs),
                    lambda e, d, p2=par2, W_=wqw: e.activation(
                        d, h_sp[p2][:, :W_], AF.Sign),
                    [f"h_sp{par2}"],
                    qh_all[:, 0:ntl, soff:soff + slen],
                    f"qhs{sp}g{g}")
            for t in tiles:
                gsl = t % GROUP
                for sp in range(NSPAN):
                    soff, slen = SPANS[sp]
                    nch = slen // P
                    transposes(
                        lambda i, g_=gsl, o=soff: qh_all[:, g_, o + i * P:
                                                         o + (i + 1) * P],
                        nch,
                        lambda i0, n, b=hqT[gsl], k0=soff // P:
                            b[:, k0 + i0:k0 + i0 + n, :].rearrange(
                                "p c f -> p (c f)"),
                        [f"qhs{sp}g{g}"], [f"hqT{gsl}s{sp}"], 6 + (sp % 2))
            for t in tiles:
                gsl = t % GROUP
                hk = [f"hqT{gsl}s{sp}" for sp in range(NSPAN)]
                for k in range(NKC):
                    S("tensor", lambda e, k_=k, b=hqT[gsl]: e.matmul(
                        pb[4][:], b[:, k_, :], wdT[:, k_, 0:512],
                        start=(k_ == 0), stop=(k_ == NKC - 1)),
                      reads=hk + [wdT_key], writes=["pb4"])
                    S("tensor", lambda e, k_=k, b=hqT[gsl]: e.matmul(
                        pb[5][:], b[:, k_, :], wdT[:, k_, 512:1024],
                        start=(k_ == 0), stop=(k_ == NKC - 1)),
                      reads=hk + [wdT_key], writes=["pb5"])
                for half in range(2):
                    bk = 4 + half
                    S("vector", lambda e, t_=t, h_=half, b=pb[bk]: e.scalar_tensor_tensor(
                        x_res[:, t_, h_ * 512:(h_ + 1) * 512], b[:], 0.5,
                        x_res[:, t_, h_ * 512:(h_ + 1) * 512], ALU.mult, ALU.add),
                      reads=[f"pb{bk}", f"x:{t}"], writes=[f"x:{t}"])
                if out_dma:
                    S("gpsimd", lambda e, t_=t: e.dma_start(
                        out=y_out.rearrange("(t p) d -> p t d", p=P)[:, t_, :],
                        in_=x_res[:, t_, :]),
                      reads=[f"x:{t}"], sem="dOUT", amt=16)

    with nc.sbuf_tensor("A_wdT", [P, NKC, DM], BF16) as A_wdT, \
         nc.sbuf_tensor("A_wgu0", [P, 2, 8, 512], BF16) as A_wgu0, \
         nc.sbuf_tensor("A_wgu1", [P, 2, 8, 512], BF16) as A_wgu1:
        ffn_phase("A", A_wdT, [A_wgu0, A_wgu1], out_dma=(stop_after == "A"))
        s.barrier()

    if stop_after == "A":
        S("sync", lambda e: e.nop(), waits=[("dOUT", s.counts.get("dOUT", 0))])
        s.emit_all(block, sems)
        stack.close()
        return nc

    with nc.sbuf_tensor("B_wkv", [P, 8, 2048], BF16) as B_wkv, \
         nc.sbuf_tensor("B_kk0", [P, DM], BF16) as B_kk0, \
         nc.sbuf_tensor("B_kk1", [P, DM], BF16) as B_kk1, \
         nc.sbuf_tensor("B_vx0", [P, NH * 65], BF16) as B_vx0, \
         nc.sbuf_tensor("B_vx1", [P, NH * 65], BF16) as B_vx1, \
         nc.sbuf_tensor("B_kvf", [64, 1040], F32) as B_kvf, \
         nc.sbuf_tensor("zeros_row", [1, 512], BF16) as zeros_row:
        B_kk = [B_kk0, B_kk1]
        B_vx = [B_vx0, B_vx1]
        wkv_view = wqkv_in.rearrange("(c p) n -> p c n", p=P)
        S("gpsimd", lambda e: e.dma_start(
            out=B_wkv[:], in_=wkv_view[:, :, DM:3 * DM]),
          writes=["B_wkv"], sem="dWKV", amt=16)
        for i in range(nb):
            S("vector", lambda e, b=B_vx[i]: e.memset(
                b[:].rearrange("p (h c) -> p h c", c=65)[:, :, 64:65], 1.0),
              writes=[f"B_vxones{i}"])

        S("vector", lambda e: e.memset(zeros_row[:], 0.0), writes=["zeros_row"])
        for bk in range(4):
            S("tensor", lambda e, b=bk: e.matmul(
                pb[b][0:64, 0:260], zeros_row[0:1, 0:64], zeros_row[0:1, 0:260],
                start=True, stop=False),
              reads=["zeros_row"], writes=[f"pb{bk}"])

        for t in range(T):
            par = t % 2
            xk = f"x:{t}"
            x_ap = x_res[:, t, :]
            ln_stats(par, x_ap, xk)
            S("scalar", lambda e, xa=x_ap, p_=par: e.activation(
                qx[p_][:], xa, AF.Identity,
                bias=nbias[p_][:, 0:1], scale=rstd[p_][:, 0:1]),
              reads=[xk, f"rstd{par}", f"nbias{par}"], writes=[f"qx{par}p0", f"qx{par}p1"])
            transposes(
                lambda i, q_=qx[par]: q_[:, i * P:(i + 1) * P], 8,
                lambda i0, n, b=xqT[par]: b[:, i0:i0 + n, :].rearrange(
                    "p c f -> p (c f)"),
                [f"qx{par}p0", f"qx{par}p1"], [f"xqT{par}"], 6 + par)
            for ch in range(4):
                bk = 4 + (ch % 2)
                for k in range(8):
                    S("tensor", lambda e, k_=k, c_=ch, b=B_wkv, p_=pb[bk],
                      xt=xqT[par]: e.matmul(
                          p_[:], xt[:, k_, :], b[:, k_, c_ * 512:(c_ + 1) * 512],
                          start=(k_ == 0), stop=(k_ == 7)),
                      reads=[f"xqT{par}", "B_wkv"], writes=[f"pb{bk}"])
                if ch < 2:
                    S("scalar", lambda e, p_=pb[bk], pr=par: e.activation(
                        wT1[pr][:, 0:512], p_[:], AF.Relu,
                        scale=np.float32(-1.0)),
                      reads=[f"pb{bk}"], writes=[f"wT1{par}"])
                    S("scalar", lambda e, pr=par: e.activation(
                        wA[pr][:, 0:512], wT1[pr][:, 0:512], AF.Exp,
                        scale=np.float32(-1.0)),
                      reads=[f"wT1{par}"], writes=[f"wA{par}"])
                    S("vector", lambda e, p_=pb[bk], c_=ch, kk=B_kk[par], pr=par:
                      e.scalar_tensor_tensor(
                          kk[:, c_ * 512:(c_ + 1) * 512], p_[:], 0.0,
                          wA[pr][:, 0:512], ALU.max, ALU.add),
                      reads=[f"pb{bk}", f"wA{par}"], writes=[f"B_kk{par}c{ch}"])
                else:
                    h0 = (ch - 2) * 8
                    S("scalar", lambda e, p_=pb[bk], h0_=h0, vx=B_vx[par]:
                      e.activation(
                          vx[:].rearrange("p (h c) -> p h c", c=65)
                          [:, h0_:h0_ + 8, 0:64],
                          p_[:], AF.Copy),
                      reads=[f"pb{bk}"], writes=[f"B_vx{par}h{h0}"])
            for h in range(NH):
                bk = h // 4
                cc = 65 * (h % 4)
                S("tensor", lambda e, h_=h, bk_=bk, cc_=cc,
                  kk=B_kk[par], vx=B_vx[par],
                  sp_=(t == T - 1 and h % 4 == 3):
                  e.matmul(
                      pb[bk_][0:64, cc_:cc_ + 65],
                      kk[:, h_ * 64:(h_ + 1) * 64],
                      vx[:, h_ * 65:(h_ + 1) * 65],
                      start=False, stop=sp_),
                  reads=[f"B_kk{par}c0", f"B_kk{par}c1",
                         f"B_vx{par}h0", f"B_vx{par}h8", f"B_vxones{par}"],
                  writes=[f"pb{bk}"] if t == T - 1 else [],
                  waits=[],
                  )
        for bk in range(4):
            S("vector", lambda e, b=bk: e.tensor_copy(
                B_kvf[0:64, b * 260:(b + 1) * 260], pb[b][0:64, 0:260]),
              reads=[f"pb{bk}"], writes=["B_kvf"])
        S("gpsimd", lambda e: e.dma_start(out=kv_src, in_=B_kvf[:]),
          reads=["B_kvf"], sem="dKV", amt=16)
        S("gpsimd", lambda e: e.collective_compute(
            "AllReduce", ALU.add, replica_groups=replica_groups,
            ins=[kv_src.opt()], outs=[kv_dst.opt()]),
          waits=[("dKV", s.counts["dKV"])], sem="cKV", amt=1)
        s.barrier()

    if stop_after == "B":
        for t in range(T):
            S("gpsimd", lambda e, t_=t: e.dma_start(
                out=y_out.rearrange("(t p) d -> p t d", p=P)[:, t_, :],
                in_=x_res[:, t_, :]),
              reads=[f"x:{t}"], sem="dOUT", amt=16)
        S("sync", lambda e: e.nop(), waits=[("dOUT", s.counts["dOUT"])])
        s.emit_all(block, sems)
        stack.close()
        return nc

    with nc.sbuf_tensor("C_wq", [P, 8, DM], BF16) as C_wq, \
         nc.sbuf_tensor("C_wout", [P, 8, DM], BF16) as C_wout, \
         nc.sbuf_tensor("C_kv", [64, 1040], F32) as C_kv, \
         nc.sbuf_tensor("C_kvb", [64, 1040], BF16) as C_kvb, \
         nc.sbuf_tensor("C_at0", [P, DM], BF16) as C_at0, \
         nc.sbuf_tensor("C_at1", [P, DM], BF16) as C_at1, \
         nc.sbuf_tensor("aT2_0", [64, NH, P], BF16) as aT2_0, \
         nc.sbuf_tensor("aT2_1", [64, NH, P], BF16) as aT2_1, \
         nc.sbuf_tensor("rn16_0", [P, NH], F32) as rn16_0, \
         nc.sbuf_tensor("rn16_1", [P, NH], F32) as rn16_1, \
         nc.sbuf_tensor("bout_sb", [1, DM], BF16) as bout_sb:
        C_at = [C_at0, C_at1]
        aT2 = [aT2_0, aT2_1]
        rn16 = [rn16_0, rn16_1]
        S("gpsimd", lambda e: e.dma_start(out=bout_sb[:], in_=bout_in),
          writes=["bout_sb"], sem="dBOUT", amt=16)
        wq_view = wqkv_in.rearrange("(c p) n -> p c n", p=P)
        S("gpsimd", lambda e: e.dma_start(out=C_wq[:], in_=wq_view[:, :, 0:DM]),
          writes=["C_wq"], sem="dWQ", amt=16)
        wo_view = wout_in.rearrange("(c p) n -> p c n", p=P)
        S("gpsimd", lambda e: e.dma_start(out=C_wout[:], in_=wo_view),
          writes=["C_wout"], sem="dWOUT", amt=16)
        tok_ckv = ("cKV", s.counts["cKV"])
        S("sync", lambda e: e.dma_start(out=C_kv[:], in_=kv_dst),
          waits=[tok_ckv], writes=["C_kv"], sem="dKVL", amt=16)
        S("vector", lambda e: e.tensor_copy(C_kvb[:], C_kv[:]),
          reads=["C_kv"], writes=["C_kvb"])

        for t in range(T):
            par = t % 2
            xk = f"x:{t}"
            x_ap = x_res[:, t, :]
            ln_stats(par, x_ap, xk)
            S("scalar", lambda e, xa=x_ap, p_=par: e.activation(
                qx[p_][:], xa, AF.Identity,
                bias=nbias[p_][:, 0:1], scale=rstd[p_][:, 0:1]),
              reads=[xk, f"rstd{par}", f"nbias{par}"], writes=[f"qx{par}p0", f"qx{par}p1"])
            transposes(
                lambda i, q_=qx[par]: q_[:, i * P:(i + 1) * P], 8,
                lambda i0, n, b=xqT[par]: b[:, i0:i0 + n, :].rearrange(
                    "p c f -> p (c f)"),
                [f"qx{par}p0", f"qx{par}p1"], [f"xqT{par}"], 6 + par)
            for ch in range(2):
                for k in range(8):
                    S("tensor", lambda e, k_=k, c_=ch, pr=par: e.matmul(
                        pb[c_][:], xqT[pr][:, k_, :],
                        C_wq[:, k_, c_ * 512:(c_ + 1) * 512],
                        start=(k_ == 0), stop=(k_ == 7)),
                      reads=[f"xqT{par}", "C_wq"], writes=[f"pb{ch}"])
                S("scalar", lambda e, c_=ch, pr=par: e.activation(
                    wT1[pr][:, 0:512], pb[c_][:], AF.Relu,
                    scale=np.float32(-1.0)),
                  reads=[f"pb{ch}"], writes=[f"wT1{par}"])
                S("scalar", lambda e, pr=par: e.activation(
                    wA[pr][:, 0:512], wT1[pr][:, 0:512], AF.Exp,
                    scale=np.float32(-1.0)),
                  reads=[f"wT1{par}"], writes=[f"wA{par}"])
                S("vector", lambda e, c_=ch, pr=par: e.scalar_tensor_tensor(
                    qx[pr][:, c_ * 512:(c_ + 1) * 512], pb[c_][:], 0.0,
                    wA[pr][:, 0:512], ALU.max, ALU.add),
                  reads=[f"pb{ch}", f"wA{par}"], writes=[f"qx{par}p{ch}"])
            if c_upto < 2:
                continue
            for half in range(2):
                bki = 6 + half
                bank = pb[bki].ap().bitcast(BF16)
                for j in range(8):
                    h = half * 8 + j
                    S("tensor", lambda e, h_=h, j_=j, b=bank, pr=par: e.transpose(
                        b[0:64, j_ * P:(j_ + 1) * P],
                        qx[pr][:, h_ * 64:(h_ + 1) * 64], ident[:]),
                      reads=[f"qx{par}p0", f"qx{par}p1", "ident"],
                      writes=[f"pb{bki}"])
                S("vector", lambda e, hf=half, b=bank, pr=par: e.tensor_copy(
                    aT2[pr][0:64, hf * 8:(hf + 1) * 8, :].rearrange(
                        "p c f -> p (c f)"),
                    b[0:64, 0:1024]),
                  reads=[f"pb{bki}"], writes=[f"qT{par}"])
            if c_upto < 3:
                continue
            headbank = lambda h: 2 + (h // 7)
            headcol = lambda h: (h % 7) * 65
            for h in range(NH):
                bk, cc = headbank(h), headcol(h)
                S("tensor", lambda e, h_=h, bk_=bk, cc_=cc, pr=par: e.matmul(
                    pb[bk_][:, cc_:cc_ + 65],
                    aT2[pr][0:64, h_, :],
                    C_kvb[0:64, (h_ // 4) * 260 + (h_ % 4) * 65:
                          (h_ // 4) * 260 + (h_ % 4) * 65 + 65],
                    start=True, stop=True),
                  reads=[f"qT{par}", "C_kvb"], writes=[f"pb{bk}"])
            if c_upto < 4:
                continue
            for bk, h0, nh in ((2, 0, 7), (3, 7, 7), (4, 14, 2)):
                S("vector", lambda e, bk_=bk, h0_=h0, nh_=nh, pr=par: e.tensor_scalar_add(
                    rn16[pr][:, h0_:h0_ + nh_],
                    pb[bk_][:, 0:nh_ * 65].rearrange("p (h c) -> p h c", c=65)
                    [:, :, 64:65],
                    float(EPS_ATTN)),
                  reads=[f"pb{bk}"], writes=[f"rn{par}a{bk}"])
            S("vector", lambda e, pr=par: e.reciprocal(rn16[pr][:], rn16[pr][:]),
              reads=[f"rn{par}a2", f"rn{par}a3", f"rn{par}a4"],
              writes=[f"rn{par}"])
            for bk, h0, nh in ((2, 0, 7), (3, 7, 7), (4, 14, 2)):
                S("vector", lambda e, bk_=bk, h0_=h0, nh_=nh, a=C_at[par], pr=par:
                  e.tensor_tensor(
                      a[:, h0_ * 64:(h0_ + nh_) * 64].rearrange(
                          "p (h c) -> p h c", c=64),
                      pb[bk_][:, 0:nh_ * 65].rearrange("p (h c) -> p h c", c=65)
                      [:, :, 0:64],
                      rn16[pr][:, h0_:h0_ + nh_].unsqueeze(2)
                      .broadcast_to([P, nh_, 64]), ALU.mult),
                  reads=[f"pb{bk}", f"rn{par}"], writes=[f"C_at{par}b{bk}"])
            if c_upto < 5:
                continue
            transposes(
                lambda i, a=C_at[par]: a[:, i * P:(i + 1) * P], 8,
                lambda i0, n, b=xqT[par]: b[:, i0:i0 + n, :].rearrange(
                    "p c f -> p (c f)"),
                [f"C_at{par}b2", f"C_at{par}b3", f"C_at{par}b4"],
                [f"xqT{par}"], 6 + par)
            if c_upto < 6:
                continue
            for ch in range(2):
                for k in range(8):
                    S("tensor", lambda e, k_=k, c_=ch, pr=par: e.matmul(
                        pb[c_][:], xqT[pr][:, k_, :],
                        C_wout[:, k_, c_ * 512:(c_ + 1) * 512],
                        start=(k_ == 0), stop=False),
                      reads=[f"xqT{par}", "C_wout"], writes=[f"pb{ch}"])
                S("tensor", lambda e, c_=ch: e.matmul(
                    pb[c_][:], ones_col[:],
                    bout_sb[:, c_ * 512:(c_ + 1) * 512],
                    start=False, stop=True),
                  reads=["ones_col", "bout_sb"], writes=[f"pb{ch}"])
                S("vector", lambda e, c_=ch, t_=t: e.scalar_tensor_tensor(
                    x_res[:, t_, c_ * 512:(c_ + 1) * 512], pb[c_][:], 1.0,
                    x_res[:, t_, c_ * 512:(c_ + 1) * 512], ALU.mult, ALU.add),
                  reads=[f"pb{ch}", f"x:{t}"], writes=[f"x:{t}"])
        s.barrier()

    if stop_after == "C":
        for t in range(T):
            S("gpsimd", lambda e, t_=t: e.dma_start(
                out=y_out.rearrange("(t p) d -> p t d", p=P)[:, t_, :],
                in_=x_res[:, t_, :]),
              reads=[f"x:{t}"], sem="dOUT", amt=16)
        S("sync", lambda e: e.nop(), waits=[("dOUT", s.counts["dOUT"])])
        s.emit_all(block, sems)
        stack.close()
        return nc

    with nc.sbuf_tensor("D_wdT", [P, NKC, DM], BF16) as D_wdT, \
         nc.sbuf_tensor("D_wgu0", [P, 2, 8, 512], BF16) as D_wgu0, \
         nc.sbuf_tensor("D_wgu1", [P, 2, 8, 512], BF16) as D_wgu1:
        ffn_phase("D", D_wdT, [D_wgu0, D_wgu1], out_dma=True)

    S("sync", lambda e: e.nop(), waits=[("dOUT", s.counts["dOUT"])])

    s.emit_all(block, sems)
    stack.close()
    return nc


def kernel(L0_attn_x, L0_attn_Wqkv, L0_attn_Wout, L0_attn_bout,
           L0_ffn_w_gate, L0_ffn_w_up, L0_ffn_w_down):
    from concourse.bass_utils import run_bass_kernel_spmd

    x = np.ascontiguousarray(np.asarray(L0_attn_x, dtype=np.float32))
    B, N, D = x.shape
    T = (N // 2) // P
    wqkv = np.ascontiguousarray(np.asarray(L0_attn_Wqkv, dtype=np.float32))
    wout = np.ascontiguousarray(np.asarray(L0_attn_Wout, dtype=np.float32))
    bout = np.ascontiguousarray(np.asarray(L0_attn_bout, dtype=np.float32)).reshape(1, D)
    wg = np.ascontiguousarray(np.asarray(L0_ffn_w_gate, dtype=np.float32))
    wu = np.ascontiguousarray(np.asarray(L0_ffn_w_up, dtype=np.float32))
    wd = np.ascontiguousarray(np.asarray(L0_ffn_w_down, dtype=np.float32))

    nc = bass.Bass("TRN2", target_bir_lowering=False)
    build(nc, T, 8, [[0, 1], [2, 3], [4, 5], [6, 7]])

    half = N // 2
    in_maps = []
    for c in range(8):
        b, hf = c // 2, c % 2
        sl = np.ascontiguousarray(x[b, hf * half:(hf + 1) * half, :])
        in_maps.append({"x_in": sl, "wqkv": wqkv, "wout": wout, "bout": bout,
                        "wg": wg, "wu": wu, "wd": wd})

    res = run_bass_kernel_spmd(nc, in_maps, core_ids=list(range(8)))
    out = np.empty((B, N, D), dtype=np.float32)
    for c in range(8):
        b, hf = c // 2, c % 2
        out[b, hf * half:(hf + 1) * half, :] = np.asarray(
            res.results[c]["y_out"], dtype=np.float32)
    return out


# version 18
# speedup vs baseline: 1.0701x; 1.0701x over previous
import numpy as np
from contextlib import ExitStack
import concourse.bass as bass
import concourse.mybir as mybir

F32 = mybir.dt.float32
BF16 = mybir.dt.bfloat16
F16 = mybir.dt.float16
I32 = mybir.dt.int32
AF = mybir.ActivationFunctionType
ALU = mybir.AluOpType
AX = mybir.AxisListType

P = 128
DM = 1024
DI = 2816
NH = 16
DH = 64
NSPAN = 6
SPANS = [(i * 512, min(512, DI - i * 512)) for i in range(NSPAN)]
NKC = DI // P
GROUP = 2
EPS_LN = 1e-5
EPS_ATTN = 1e-6


class Sched:
    ENGS = ("sync", "gpsimd", "scalar", "vector", "tensor")
    ESEM = {"vector": "sDVE", "scalar": "sACT", "gpsimd": "sPOOL",
            "tensor": "sPE", "sync": "sSP"}

    def __init__(self):
        self.steps = {e: [] for e in self.ENGS}
        self.counts = {}
        self.writers = {}
        self.readers = {}

    def step(self, eng, emit, reads=(), writes=(), waits=(), sem="auto", amt=1):
        if sem == "auto":
            sem = self.ESEM[eng]
        wl = set(w for w in waits if w is not None)
        for r in reads:
            t = self.writers.get(r)
            if t is not None:
                wl.add(t)
        for w in writes:
            t = self.writers.get(w)
            if t is not None:
                wl.add(t)
            for t2 in self.readers.get(w, ()):
                wl.add(t2)
        c = self.counts.get(sem, 0) + amt
        self.counts[sem] = c
        tok = (sem, c)
        self.steps[eng].append((emit, sorted(wl), sem, amt))
        for r in reads:
            self.readers.setdefault(r, []).append(tok)
        for w in writes:
            self.writers[w] = tok
            self.readers[w] = []
        return tok

    def barrier(self):
        toks = []
        for e in self.ENGS:
            sem = self.ESEM[e]
            toks.append((sem, self.counts.get(sem, 0)))
        for e in self.ENGS:
            self.step(e, lambda eng: eng.nop(), waits=[t for t in toks])

    def emit_all(self, block, sems):
        for eng_name in self.ENGS:
            steps = self.steps[eng_name]

            def mk(eng_name=eng_name, steps=steps):
                def body(eng):
                    waited = {}
                    own = self.ESEM[eng_name]
                    for (emit, wl, sem, amt) in steps:
                        for (s_, c_) in wl:
                            if s_ == own and eng_name == "tensor":
                                continue
                            if waited.get(s_, -1) < c_:
                                eng.wait_ge(sems[s_], c_)
                                waited[s_] = c_
                        inst = emit(eng)
                        if inst is not None:
                            inst.then_inc(sems[sem], amt)
                return body
            getattr(block, eng_name)(mk())




def build(nc, T, n_cores_collective, replica_groups, dbg=False, stop_after="Z",
          c_upto=9):
    TOK = T * P
    stack = ExitStack()
    en = stack.enter_context
    en(nc.allow_low_precision(
        "f16 tiles only carry fp4-quantization decisions / staircase sums; "
        "validated numerically with 4x margin vs the tolerance"))

    x_in = nc.dram_tensor("x_in", [TOK, DM], F32, kind="ExternalInput").ap()
    wqkv_in = nc.dram_tensor("wqkv", [DM, 3 * DM], F32, kind="ExternalInput").ap()
    wout_in = nc.dram_tensor("wout", [DM, DM], F32, kind="ExternalInput").ap()
    bout_in = nc.dram_tensor("bout", [1, DM], F32, kind="ExternalInput").ap()
    wg_in = nc.dram_tensor("wg", [DI, DM], F32, kind="ExternalInput").ap()
    wu_in = nc.dram_tensor("wu", [DI, DM], F32, kind="ExternalInput").ap()
    wd_in = nc.dram_tensor("wd", [DM, DI], F32, kind="ExternalInput").ap()
    y_out = nc.dram_tensor("y_out", [TOK, DM], F32, kind="ExternalOutput").ap()

    wgu_spill = nc.dram_tensor("wgu_spill", [NSPAN, P, 2, 8, 512], BF16).ap()
    wd_spill = nc.dram_tensor("wd_spill", [P, NKC, DM], BF16).ap()
    kv_src = nc.dram_tensor("kv_src", [64, 1040], F32).ap()
    kv_dst = nc.dram_tensor("kv_dst", [64, 1040], F32).ap()

    s = Sched()
    S = s.step

    x_res = en(nc.sbuf_tensor("x_res", [P, T, DM], BF16))
    ident = en(nc.sbuf_tensor("ident", [P, P], BF16))

    ones_col = en(nc.sbuf_tensor("ones_col", [1, P], BF16))


    nb = 2
    nq = 3
    WQ = GROUP * 512
    wA = [en(nc.sbuf_tensor(f"wA{i}", [P, WQ], F16)) for i in range(nq)]
    wG = [en(nc.sbuf_tensor(f"wG{i}", [P, WQ], F16)) for i in range(nq)]
    wT1 = [en(nc.sbuf_tensor(f"wT1{i}", [P, WQ], F16)) for i in range(nq)]
    wSS = [en(nc.sbuf_tensor(f"wSS{i}", [P, WQ], F16)) for i in range(nq)]
    h_sp = [en(nc.sbuf_tensor(f"h_sp{i}", [P, WQ], F16)) for i in range(nq)]
    wU = h_sp
    g_gm = [en(nc.sbuf_tensor(f"g_gm{i}", [P, 64], F16)) for i in range(nq)]
    g_r = [en(nc.sbuf_tensor(f"g_r{i}", [P, 64], F16)) for i in range(nq)]
    g_sh = [en(nc.sbuf_tensor(f"g_sh{i}", [P, 64], F16)) for i in range(nq)]
    qx = [en(nc.sbuf_tensor(f"qx{i}", [P, DM], BF16)) for i in range(nb)]
    xqT = [en(nc.sbuf_tensor(f"xqT{i}", [P, 8, P], BF16)) for i in range(max(GROUP, nb))]

    hqT = [en(nc.sbuf_tensor(f"hqT{i}", [P, NKC, P], BF16)) for i in range(GROUP)]
    qh_all = en(nc.sbuf_tensor("qh_all", [P, GROUP, DI], BF16))
    lnsum = [en(nc.sbuf_tensor(f"lnsum{i}", [P, 1], F32)) for i in range(nb)]
    lnss = [en(nc.sbuf_tensor(f"lnss{i}", [P, 1], F32)) for i in range(nb)]
    lnm = [en(nc.sbuf_tensor(f"lnm{i}", [P, 1], F32)) for i in range(nb)]
    lnv = [en(nc.sbuf_tensor(f"lnv{i}", [P, 1], F32)) for i in range(nb)]
    rstd = [en(nc.sbuf_tensor(f"rstd{i}", [P, 1], F32)) for i in range(nb)]
    nbias = [en(nc.sbuf_tensor(f"nbias{i}", [P, 1], F32)) for i in range(nb)]



    pb = [en(nc.psum_tensor(f"pb{i}", [P, 512], F32)) for i in range(8)]

    sem_names = ["sDVE", "sACT", "sPOOL", "sPE", "sSP",
                 "dXIN", "dW0", "dW1", "dMISC", "dSPILL", "dOUT", "dKV", "cKV",
                 "dBOUT", "dWKV", "dWQ", "dWOUT", "dKVL", "dWDT", "dSPILL1", "dMISC1", "dWU0", "dWU1"]
    sems = {n: en(nc.semaphore(n)) for n in sem_names}

    block = en(nc.Block())

    def view16(ap, w):
        return ap.rearrange("p (g w) -> p g w", w=16)

    def bc16(small_ap, g16):
        return small_ap.unsqueeze(2).broadcast_to([P, g16, 16])

    def quant_chain(par, width, src_emit_abs, src_emit_sgn, src_reads,
                    q_dst_ap, q_write_res, extra_waits=(), B=None):
        g16 = width // 16
        if B is None:
            A_, G_, U_, T1, SS = wA[par], wG[par], wU[par], wT1[par], wSS[par]
            T2 = A_
            GM, R_, SH = g_gm[par], g_r[par], g_sh[par]
            ra = rt2 = f"wA{par}"
            rg, ru, rt1, rss = f"wG{par}", f"h_sp{par}", f"wT1{par}", f"wSS{par}"
            rgm, rr, rsh = f"g_gm{par}", f"g_r{par}", f"g_sh{par}"
        else:
            bufs, kp = B
            A_, G_, U_, T1, T2, SS, GM, SC_, R_, SH = (b[par] for b in bufs)
            ra, rg, ru, rt1, rt2, rss = (f"{kp}a{par}", f"{kp}g{par}",
                                         f"{kp}u{par}", f"{kp}t1{par}",
                                         f"{kp}t2{par}", f"{kp}ss{par}")
            rgm, rr, rsh = f"{kp}gm{par}", f"{kp}r{par}", f"{kp}sh{par}"

        S("scalar", lambda e: src_emit_abs(e, A_[:, :width]),
          reads=src_reads, writes=[ra], waits=extra_waits)
        S("scalar", lambda e: src_emit_sgn(e, G_[:, :width]),
          reads=src_reads, writes=[rg], waits=extra_waits)
        S("vector", lambda e: e.tensor_reduce(
            GM[:, :g16], view16(A_[:, :width], width), AX.X, ALU.max),
          reads=[ra], writes=[rgm])
        S("vector", lambda e: e.tensor_scalar(
            SH[:, :g16], GM[:, :g16], 1.0 / 12.0, 5e-13, ALU.mult, ALU.max),
          reads=[rgm], writes=[rsh])
        S("vector", lambda e: e.reciprocal(R_[:, :g16], GM[:, :g16]),
          reads=[rgm], writes=[rr])
        S("vector", lambda e: e.tensor_scalar(
            R_[:, :g16], R_[:, :g16], 6.0, 1e12, ALU.mult, ALU.min),
          reads=[rr], writes=[rr])
        S("gpsimd", lambda e: e.tensor_tensor(
            view16(U_[:, :width], width), view16(A_[:, :width], width),
            bc16(R_[:, :g16], g16), ALU.mult),
          reads=[ra, rr], writes=[ru])
        S("gpsimd", lambda e: e.tensor_tensor(
            view16(SS[:, :width], width), view16(G_[:, :width], width),
            bc16(SH[:, :g16], g16), ALU.mult),
          reads=[rg, rsh], writes=[rss])
        S("vector", lambda e: e.tensor_scalar(
            T1[:, :width], U_[:, :width], 5.0, 2.0, ALU.is_gt, ALU.mult),
          reads=[ru], writes=[rt1])
        S("vector", lambda e: e.scalar_tensor_tensor(
            T2[:, :width], U_[:, :width], 3.5, T1[:, :width], ALU.is_gt, ALU.add),
          reads=[ru, rt1], writes=[rt2])
        S("vector", lambda e: e.scalar_tensor_tensor(
            T1[:, :width], U_[:, :width], 2.5, T2[:, :width], ALU.is_gt, ALU.add),
          reads=[ru, rt2], writes=[rt1])
        S("vector", lambda e: e.tensor_scalar_mul(
            T2[:, :width], T1[:, :width], 2.0),
          reads=[rt1], writes=[rt2])
        S("vector", lambda e: e.scalar_tensor_tensor(
            T1[:, :width], U_[:, :width], 1.75, T2[:, :width], ALU.is_gt, ALU.add),
          reads=[ru, rt2], writes=[rt1])
        S("vector", lambda e: e.scalar_tensor_tensor(
            T2[:, :width], U_[:, :width], 1.25, T1[:, :width], ALU.is_gt, ALU.add),
          reads=[ru, rt1], writes=[rt2])
        S("vector", lambda e: e.scalar_tensor_tensor(
            T1[:, :width], U_[:, :width], 0.75, T2[:, :width], ALU.is_gt, ALU.add),
          reads=[ru, rt2], writes=[rt1])
        S("vector", lambda e: e.scalar_tensor_tensor(
            T2[:, :width], U_[:, :width], 0.25, T1[:, :width], ALU.is_gt, ALU.add),
          reads=[ru, rt1], writes=[rt2])
        tq = S("gpsimd", lambda e: e.tensor_mul(
            q_dst_ap, T2[:, :width], SS[:, :width]),
          reads=[rt2, rss], writes=[q_write_res])
        return tq

    def transposes(src_fn, nchunks, dst_fn, src_reads, dst_writes, bank_idx):
        i = 0
        while i < nchunks:
            n = min(8, nchunks - i)
            bank = pb[bank_idx].ap().bitcast(BF16)
            for j in range(n):
                S("tensor", lambda e, jj=i + j, oj=j, b=bank: e.transpose(
                    b[:, oj * P:(oj + 1) * P], src_fn(jj), ident[:]),
                  reads=src_reads + ["ident"], writes=[f"pb{bank_idx}"])
            S("scalar", lambda e, ii=i, n_=n, b=bank: e.copy(
                dst_fn(ii, n_), b[:, :n_ * P]),
              reads=[f"pb{bank_idx}"], writes=dst_writes)
            i += n

    def ln_stats(par, x_ap, x_res_key):
        S("scalar", lambda e: e.activation(
            qx[par][:], x_ap, AF.Identity, accum_out=lnsum[par][:]),
          reads=[x_res_key],
          writes=[f"lnsum{par}", f"qx{par}p0", f"qx{par}p1"])
        S("scalar", lambda e: e.activation(
            qx[par][:], x_ap, AF.Square, accum_out=lnss[par][:]),
          reads=[x_res_key],
          writes=[f"lnss{par}", f"qx{par}p0", f"qx{par}p1"])
        S("vector", lambda e: e.tensor_scalar_mul(
            lnm[par][:], lnsum[par][:], 1.0 / DM),
          reads=[f"lnsum{par}"], writes=[f"lnm{par}"])
        S("vector", lambda e: e.tensor_scalar_mul(
            lnv[par][:], lnss[par][:], 1.0 / DM),
          reads=[f"lnss{par}"], writes=[f"lnv{par}"])
        S("vector", lambda e: e.scalar_tensor_tensor(
            lnv[par][:], lnm[par][:], lnm[par][:, 0:1], lnv[par][:],
            ALU.mult, ALU.subtract),
          reads=[f"lnm{par}", f"lnv{par}"], writes=[f"lnv{par}"])
        S("scalar", lambda e: e.activation(
            rstd[par][:], lnv[par][:], AF.Ln, bias=np.float32(EPS_LN), scale=np.float32(-1.0)),
          reads=[f"lnv{par}"], writes=[f"rstd{par}"])
        S("scalar", lambda e: e.activation(
            rstd[par][:], rstd[par][:], AF.Exp, scale=np.float32(-0.5)),
          reads=[f"rstd{par}"], writes=[f"rstd{par}"])
        S("vector", lambda e: e.scalar_tensor_tensor(
            nbias[par][:], lnm[par][:], -1.0, rstd[par][:], ALU.mult, ALU.mult),
          reads=[f"lnm{par}", f"rstd{par}"], writes=[f"nbias{par}"])

    sustack = ExitStack()
    identi = sustack.enter_context(nc.sbuf_tensor("identi", [P, P], I32))
    wstage = [sustack.enter_context(nc.sbuf_tensor(f"wstage{i}", [P, DM], F32))
              for i in range(2)]

    x_view = x_in.rearrange("(t p) d -> p t d", p=P)
    nxd = 4 if T >= 4 else 1
    step_t = (T + nxd - 1) // nxd
    for i in range(0, T, step_t):
        n = min(step_t, T - i)
        S("gpsimd", lambda e, i_=i, n_=n: e.dma_start(
            out=x_res[:, i_:i_ + n_, :], in_=x_view[:, i_:i_ + n_, :]),
          writes=[f"x:{t}" for t in range(i, i + n)], sem="dXIN", amt=16)
    tok_x_all = ("dXIN", s.counts["dXIN"])
    S("sync", lambda e: e.nop(), waits=[tok_x_all],
      writes=[f"x:{t}" for t in range(T)])

    S("gpsimd", lambda e: e.iota(identi[:], pattern=[[1, P]], base=0,
                                 channel_multiplier=-1), writes=["identi"])
    S("gpsimd", lambda e: e.tensor_scalar(
        ident[:], identi[:], 0.0, None, ALU.is_equal),
      reads=["identi"], writes=["ident"])
    S("vector", lambda e: e.memset(ones_col[:], 1.0), writes=["ones_col"])

    spill_tok = {}
    wgv = wg_in.rearrange("(c p) d -> c p d", p=P)
    wuv = wu_in.rearrange("(c p) d -> c p d", p=P)
    for c in range(NKC):
        for mi, wv in ((0, wgv[c]), (1, wuv[c])):
            par = (2 * c + mi) % nq
            wpar = (2 * c + mi) % 2
            ws = wstage[wpar]
            wsk = f"wstage{wpar}"
            S("sync", lambda e, wv_=wv, w_=ws: e.dma_start(out=w_[:], in_=wv_),
              writes=[wsk], sem=("dMISC" if wpar == 0 else "dMISC1"), amt=16)
            qpar = wpar
            quant_chain(
                par, DM,
                lambda e, d, w_=ws: e.activation(d, w_[:], AF.Abs),
                lambda e, d, w_=ws: e.activation(d, w_[:], AF.Sign),
                [wsk],
                qx[qpar][:], f"qx{qpar}p0")
            transposes(
                lambda i, q_=qx[qpar]: q_[:, i * P:(i + 1) * P], 8,
                lambda i0_, n, b=xqT[qpar]: b[:, i0_:i0_ + n, :].rearrange(
                    "p c f -> p (c f)"),
                [f"qx{qpar}p0"], [f"xqT{qpar}"], 6 + qpar)
            sp_, off_ = (c * P) // 512, (c * P) % 512
            spill_tok[(mi, c)] = S(
                "gpsimd", lambda e, s_=sp_, o_=off_, b=xqT[qpar], si=mi:
                e.dma_start(out=wgu_spill[s_, :, si, :, o_:o_ + P], in_=b[:]),
                reads=[f"xqT{qpar}"],
                sem=("dSPILL" if qpar == 0 else "dSPILL1"), amt=16)

    with ExitStack() as wstack:
        wen = wstack.enter_context
        w32 = tuple(
            [wen(nc.sbuf_tensor(f"w32{nm}{i}", [P, w_], F32)) for i in range(nb)]
            for nm, w_ in (("a", 1024), ("g", 1024), ("u", 1024), ("t1", 1024),
                           ("t2", 1024), ("ss", 1024), ("gm", 64), ("sc", 64),
                           ("r", 64), ("sh", 64)))
        B32 = (w32, "w32")
        wdv = wd_in.rearrange("(c p) d -> c p d", p=P)
        for c in range(8):
            for piece in range(3):
                poff = piece * 1024
                plen = min(1024, DI - poff)
                par = (3 * c + piece) % 2
                ws = wstage[par]
                wsk = f"wstage{par}"
                S("sync", lambda e, wv_=wdv[c], o=poff, L=plen, w_=ws:
                  e.dma_start(out=w_[:, 0:L], in_=wv_[:, o:o + L]),
                  writes=[wsk], sem=("dMISC" if par == 0 else "dMISC1"), amt=16)
                quant_chain(
                    par, plen,
                    lambda e, d, L=plen, w_=ws: e.activation(
                        d, w_[:, 0:L], AF.Abs),
                    lambda e, d, L=plen, w_=ws: e.activation(
                        d, w_[:, 0:L], AF.Sign),
                    [wsk],
                    qx[par][:, :plen], f"qx{par}p0", B=B32)
                nch = plen // P
                transposes(
                    lambda i, q_=qx[par]: q_[:, i * P:(i + 1) * P], nch,
                    lambda i0_, n, b=xqT[par]: b[:, i0_:i0_ + n, :].rearrange(
                        "p c f -> p (c f)"),
                    [f"qx{par}p0"], [f"xqT{par}"], 6 + par)
                spill_tok[(2, 3 * c + piece)] = S(
                    "gpsimd", lambda e, b=xqT[par], k0=poff // P, n=nch, cc=c:
                    e.dma_start(out=wd_spill[:, k0:k0 + n, cc * P:(cc + 1) * P],
                                in_=b[:, 0:n, :]),
                    reads=[f"xqT{par}"],
                    sem=("dSPILL" if par == 0 else "dSPILL1"), amt=16)

    tok_spill_all = ("dSPILL", s.counts["dSPILL"])
    tok_spill_all2 = ("dSPILL1", s.counts["dSPILL1"])
    s.barrier()
    sustack.close()

    def ffn_phase(phase, wdT, wgu, out_dma=False):
        wdT_key = f"{phase}:wdT"
        S("sync", lambda e: e.dma_start(out=wdT[:], in_=wd_spill),
          waits=[spill_tok[(2, i_)] for i_ in range(24)], writes=[wdT_key],
          sem="dWDT", amt=16)

        ngroups = (T + GROUP - 1) // GROUP
        span_loads = {}
        for g in range(ngroups):
            tiles = list(range(g * GROUP, min(T, (g + 1) * GROUP)))
            for t in tiles:
                par = t % 2
                gsl = t % GROUP
                xk = f"x:{t}"
                x_ap = x_res[:, t, :]
                ln_stats(par, x_ap, xk)
                quant_chain(
                    t % nq, DM,
                    lambda e, d, xa=x_ap, p_=par: e.activation(
                        d, xa, AF.Abs,
                        bias=nbias[p_][:, 0:1], scale=rstd[p_][:, 0:1]),
                    lambda e, d, xa=x_ap, p_=par: e.activation(
                        d, xa, AF.Sign,
                        bias=nbias[p_][:, 0:1], scale=rstd[p_][:, 0:1]),
                    [xk, f"rstd{par}", f"nbias{par}"],
                    qx[par][:], f"qx{par}p0")
                transposes(
                    lambda i, q_=qx[par]: q_[:, i * P:(i + 1) * P], 8,
                    lambda i0, n, b=xqT[gsl]: b[:, i0:i0 + n, :].rearrange(
                        "p c f -> p (c f)"),
                    [f"qx{par}p0"], [f"xqT{gsl}"], 6 + par)

            pend = []
            for sp in range(NSPAN):
                soff, slen = SPANS[sp]
                buf = sp % 2
                wk = f"{phase}:wgu{buf}"
                dsem = "dW0" if buf == 0 else "dW1"
                wkg, wku = wk + "g", wk + "u"
                dsem_u = "dWU0" if buf == 0 else "dWU1"
                cw = range(sp * 4, min(NKC, sp * 4 + 4))
                S("sync", lambda e, sp_=sp, b=wgu[buf], L=slen: e.dma_start(
                    out=b[:, 0, :, 0:L], in_=wgu_spill[sp_][:, 0, :, 0:L]),
                  waits=[spill_tok[(0, c_)] for c_ in cw], writes=[wkg],
                  sem=dsem, amt=16)
                S("sync", lambda e, sp_=sp, b=wgu[buf], L=slen: e.dma_start(
                    out=b[:, 1, :, 0:L], in_=wgu_spill[sp_][:, 1, :, 0:L]),
                  waits=[spill_tok[(1, c_)] for c_ in cw], writes=[wku],
                  sem=dsem_u, amt=16)
                par2 = sp % nq
                ntl = len(tiles)
                for t in tiles:
                    gsl = t % GROUP
                    it = sp * GROUP + gsl
                    pp2 = it % 2
                    bg, bu = (0, 1) if pp2 == 0 else (2, 3)
                    for k in range(8):
                        S("tensor", lambda e, k_=k, b=wgu[buf], pg=pb[bg],
                          xt=xqT[gsl], L=slen: e.matmul(
                              pg[:, :L], xt[:, k_, :], b[:, 0, k_, 0:L],
                              start=(k_ == 0), stop=(k_ == 7)),
                          reads=[f"xqT{gsl}", wkg], writes=[f"pb{bg}"])
                    for k in range(8):
                        S("tensor", lambda e, k_=k, b=wgu[buf], pu=pb[bu],
                          xt=xqT[gsl], L=slen: e.matmul(
                              pu[:, :L], xt[:, k_, :], b[:, 1, k_, 0:L],
                              start=(k_ == 0), stop=(k_ == 7)),
                          reads=[f"xqT{gsl}", wku], writes=[f"pb{bu}"])
                    S("scalar", lambda e, pg=pb[bg], L=slen, p2=par2, g_=gsl:
                      e.activation(
                          wA[p2][:, g_ * L:(g_ + 1) * L], pg[:, :L], AF.Silu),
                      reads=[f"pb{bg}"], writes=[f"wA{par2}"])
                    S("vector", lambda e, pu=pb[bu], L=slen, p2=par2, g_=gsl:
                      e.tensor_mul(
                          h_sp[p2][:, g_ * L:(g_ + 1) * L],
                          wA[p2][:, g_ * L:(g_ + 1) * L], pu[:, :L]),
                      reads=[f"wA{par2}", f"pb{bu}"], writes=[f"h_sp{par2}"])
                pend.append((par2, ntl * slen, sp, soff, slen))
                if len(pend) > 1:
                    p2_, W_, sp_, so_, sl_ = pend.pop(0)
                    quant_chain(
                        p2_, W_,
                        lambda e, d, p2=p2_, W2=W_: e.activation(
                            d, h_sp[p2][:, :W2], AF.Abs),
                        lambda e, d, p2=p2_, W2=W_: e.activation(
                            d, h_sp[p2][:, :W2], AF.Sign),
                        [f"h_sp{p2_}"],
                        qh_all[:, 0:ntl, so_:so_ + sl_],
                        f"qhs{sp_}g{g}")
            while pend:
                p2_, W_, sp_, so_, sl_ = pend.pop(0)
                ntl = len(tiles)
                quant_chain(
                    p2_, W_,
                    lambda e, d, p2=p2_, W2=W_: e.activation(
                        d, h_sp[p2][:, :W2], AF.Abs),
                    lambda e, d, p2=p2_, W2=W_: e.activation(
                        d, h_sp[p2][:, :W2], AF.Sign),
                    [f"h_sp{p2_}"],
                    qh_all[:, 0:ntl, so_:so_ + sl_],
                    f"qhs{sp_}g{g}")
            for t in tiles:
                gsl = t % GROUP
                for sp in range(NSPAN):
                    soff, slen = SPANS[sp]
                    nch = slen // P
                    transposes(
                        lambda i, g_=gsl, o=soff: qh_all[:, g_, o + i * P:
                                                         o + (i + 1) * P],
                        nch,
                        lambda i0, n, b=hqT[gsl], k0=soff // P:
                            b[:, k0 + i0:k0 + i0 + n, :].rearrange(
                                "p c f -> p (c f)"),
                        [f"qhs{sp}g{g}"], [f"hqT{gsl}s{sp}"], 6 + (sp % 2))
            for t in tiles:
                gsl = t % GROUP
                hk = [f"hqT{gsl}s{sp}" for sp in range(NSPAN)]
                for k in range(NKC):
                    S("tensor", lambda e, k_=k, b=hqT[gsl]: e.matmul(
                        pb[4][:], b[:, k_, :], wdT[:, k_, 0:512],
                        start=(k_ == 0), stop=(k_ == NKC - 1)),
                      reads=hk + [wdT_key], writes=["pb4"])
                    S("tensor", lambda e, k_=k, b=hqT[gsl]: e.matmul(
                        pb[5][:], b[:, k_, :], wdT[:, k_, 512:1024],
                        start=(k_ == 0), stop=(k_ == NKC - 1)),
                      reads=hk + [wdT_key], writes=["pb5"])
                for half in range(2):
                    bk = 4 + half
                    S("vector", lambda e, t_=t, h_=half, b=pb[bk]: e.scalar_tensor_tensor(
                        x_res[:, t_, h_ * 512:(h_ + 1) * 512], b[:], 0.5,
                        x_res[:, t_, h_ * 512:(h_ + 1) * 512], ALU.mult, ALU.add),
                      reads=[f"pb{bk}", f"x:{t}"], writes=[f"x:{t}"])
                if out_dma:
                    S("gpsimd", lambda e, t_=t: e.dma_start(
                        out=y_out.rearrange("(t p) d -> p t d", p=P)[:, t_, :],
                        in_=x_res[:, t_, :]),
                      reads=[f"x:{t}"], sem="dOUT", amt=16)

    with nc.sbuf_tensor("A_wdT", [P, NKC, DM], BF16) as A_wdT, \
         nc.sbuf_tensor("A_wgu0", [P, 2, 8, 512], BF16) as A_wgu0, \
         nc.sbuf_tensor("A_wgu1", [P, 2, 8, 512], BF16) as A_wgu1:
        ffn_phase("A", A_wdT, [A_wgu0, A_wgu1], out_dma=(stop_after == "A"))
        s.barrier()

    if stop_after == "A":
        S("sync", lambda e: e.nop(), waits=[("dOUT", s.counts.get("dOUT", 0))])
        s.emit_all(block, sems)
        stack.close()
        return nc

    with nc.sbuf_tensor("B_wkv", [P, 8, 2048], BF16) as B_wkv, \
         nc.sbuf_tensor("B_kk0", [P, DM], BF16) as B_kk0, \
         nc.sbuf_tensor("B_kk1", [P, DM], BF16) as B_kk1, \
         nc.sbuf_tensor("B_vx0", [P, NH * 65], BF16) as B_vx0, \
         nc.sbuf_tensor("B_vx1", [P, NH * 65], BF16) as B_vx1, \
         nc.sbuf_tensor("B_kvf", [64, 1040], F32) as B_kvf, \
         nc.sbuf_tensor("zeros_row", [1, 512], BF16) as zeros_row:
        B_kk = [B_kk0, B_kk1]
        B_vx = [B_vx0, B_vx1]
        wkv_view = wqkv_in.rearrange("(c p) n -> p c n", p=P)
        S("gpsimd", lambda e: e.dma_start(
            out=B_wkv[:], in_=wkv_view[:, :, DM:3 * DM]),
          writes=["B_wkv"], sem="dWKV", amt=16)
        for i in range(nb):
            S("vector", lambda e, b=B_vx[i]: e.memset(
                b[:].rearrange("p (h c) -> p h c", c=65)[:, :, 64:65], 1.0),
              writes=[f"B_vxones{i}"])

        S("vector", lambda e: e.memset(zeros_row[:], 0.0), writes=["zeros_row"])
        for bk in range(4):
            S("tensor", lambda e, b=bk: e.matmul(
                pb[b][0:64, 0:260], zeros_row[0:1, 0:64], zeros_row[0:1, 0:260],
                start=True, stop=False),
              reads=["zeros_row"], writes=[f"pb{bk}"])

        for t in range(T):
            par = t % 2
            xk = f"x:{t}"
            x_ap = x_res[:, t, :]
            ln_stats(par, x_ap, xk)
            S("scalar", lambda e, xa=x_ap, p_=par: e.activation(
                qx[p_][:], xa, AF.Identity,
                bias=nbias[p_][:, 0:1], scale=rstd[p_][:, 0:1]),
              reads=[xk, f"rstd{par}", f"nbias{par}"], writes=[f"qx{par}p0", f"qx{par}p1"])
            transposes(
                lambda i, q_=qx[par]: q_[:, i * P:(i + 1) * P], 8,
                lambda i0, n, b=xqT[par]: b[:, i0:i0 + n, :].rearrange(
                    "p c f -> p (c f)"),
                [f"qx{par}p0", f"qx{par}p1"], [f"xqT{par}"], 6 + par)
            for ch in range(4):
                bk = 4 + (ch % 2)
                for k in range(8):
                    S("tensor", lambda e, k_=k, c_=ch, b=B_wkv, p_=pb[bk],
                      xt=xqT[par]: e.matmul(
                          p_[:], xt[:, k_, :], b[:, k_, c_ * 512:(c_ + 1) * 512],
                          start=(k_ == 0), stop=(k_ == 7)),
                      reads=[f"xqT{par}", "B_wkv"], writes=[f"pb{bk}"])
                if ch < 2:
                    S("scalar", lambda e, p_=pb[bk], pr=par: e.activation(
                        wT1[pr][:, 0:512], p_[:], AF.Relu,
                        scale=np.float32(-1.0)),
                      reads=[f"pb{bk}"], writes=[f"wT1{par}"])
                    S("scalar", lambda e, pr=par: e.activation(
                        wA[pr][:, 0:512], wT1[pr][:, 0:512], AF.Exp,
                        scale=np.float32(-1.0)),
                      reads=[f"wT1{par}"], writes=[f"wA{par}"])
                    S("vector", lambda e, p_=pb[bk], c_=ch, kk=B_kk[par], pr=par:
                      e.scalar_tensor_tensor(
                          kk[:, c_ * 512:(c_ + 1) * 512], p_[:], 0.0,
                          wA[pr][:, 0:512], ALU.max, ALU.add),
                      reads=[f"pb{bk}", f"wA{par}"], writes=[f"B_kk{par}c{ch}"])
                else:
                    h0 = (ch - 2) * 8
                    S("scalar", lambda e, p_=pb[bk], h0_=h0, vx=B_vx[par]:
                      e.activation(
                          vx[:].rearrange("p (h c) -> p h c", c=65)
                          [:, h0_:h0_ + 8, 0:64],
                          p_[:], AF.Copy),
                      reads=[f"pb{bk}"], writes=[f"B_vx{par}h{h0}"])
            for h in range(NH):
                bk = h // 4
                cc = 65 * (h % 4)
                S("tensor", lambda e, h_=h, bk_=bk, cc_=cc,
                  kk=B_kk[par], vx=B_vx[par],
                  sp_=(t == T - 1 and h % 4 == 3):
                  e.matmul(
                      pb[bk_][0:64, cc_:cc_ + 65],
                      kk[:, h_ * 64:(h_ + 1) * 64],
                      vx[:, h_ * 65:(h_ + 1) * 65],
                      start=False, stop=sp_),
                  reads=[f"B_kk{par}c0", f"B_kk{par}c1",
                         f"B_vx{par}h0", f"B_vx{par}h8", f"B_vxones{par}"],
                  writes=[f"pb{bk}"] if t == T - 1 else [],
                  waits=[],
                  )
        for bk in range(4):
            S("vector", lambda e, b=bk: e.tensor_copy(
                B_kvf[0:64, b * 260:(b + 1) * 260], pb[b][0:64, 0:260]),
              reads=[f"pb{bk}"], writes=["B_kvf"])
        S("gpsimd", lambda e: e.dma_start(out=kv_src, in_=B_kvf[:]),
          reads=["B_kvf"], sem="dKV", amt=16)
        S("gpsimd", lambda e: e.collective_compute(
            "AllReduce", ALU.add, replica_groups=replica_groups,
            ins=[kv_src.opt()], outs=[kv_dst.opt()]),
          waits=[("dKV", s.counts["dKV"])], sem="cKV", amt=1)
        s.barrier()

    if stop_after == "B":
        for t in range(T):
            S("gpsimd", lambda e, t_=t: e.dma_start(
                out=y_out.rearrange("(t p) d -> p t d", p=P)[:, t_, :],
                in_=x_res[:, t_, :]),
              reads=[f"x:{t}"], sem="dOUT", amt=16)
        S("sync", lambda e: e.nop(), waits=[("dOUT", s.counts["dOUT"])])
        s.emit_all(block, sems)
        stack.close()
        return nc

    with nc.sbuf_tensor("C_wq", [P, 8, DM], BF16) as C_wq, \
         nc.sbuf_tensor("C_wout", [P, 8, DM], BF16) as C_wout, \
         nc.sbuf_tensor("C_kv", [64, 1040], F32) as C_kv, \
         nc.sbuf_tensor("C_kvb", [64, 1040], BF16) as C_kvb, \
         nc.sbuf_tensor("C_at0", [P, DM], BF16) as C_at0, \
         nc.sbuf_tensor("C_at1", [P, DM], BF16) as C_at1, \
         nc.sbuf_tensor("aT2_0", [64, NH, P], BF16) as aT2_0, \
         nc.sbuf_tensor("aT2_1", [64, NH, P], BF16) as aT2_1, \
         nc.sbuf_tensor("rn16_0", [P, NH], F32) as rn16_0, \
         nc.sbuf_tensor("rn16_1", [P, NH], F32) as rn16_1, \
         nc.sbuf_tensor("bout_sb", [1, DM], BF16) as bout_sb:
        C_at = [C_at0, C_at1]
        aT2 = [aT2_0, aT2_1]
        rn16 = [rn16_0, rn16_1]
        S("gpsimd", lambda e: e.dma_start(out=bout_sb[:], in_=bout_in),
          writes=["bout_sb"], sem="dBOUT", amt=16)
        wq_view = wqkv_in.rearrange("(c p) n -> p c n", p=P)
        S("gpsimd", lambda e: e.dma_start(out=C_wq[:], in_=wq_view[:, :, 0:DM]),
          writes=["C_wq"], sem="dWQ", amt=16)
        wo_view = wout_in.rearrange("(c p) n -> p c n", p=P)
        S("gpsimd", lambda e: e.dma_start(out=C_wout[:], in_=wo_view),
          writes=["C_wout"], sem="dWOUT", amt=16)
        tok_ckv = ("cKV", s.counts["cKV"])
        S("sync", lambda e: e.dma_start(out=C_kv[:], in_=kv_dst),
          waits=[tok_ckv], writes=["C_kv"], sem="dKVL", amt=16)
        S("vector", lambda e: e.tensor_copy(C_kvb[:], C_kv[:]),
          reads=["C_kv"], writes=["C_kvb"])

        for t in range(T):
            par = t % 2
            xk = f"x:{t}"
            x_ap = x_res[:, t, :]
            ln_stats(par, x_ap, xk)
            S("scalar", lambda e, xa=x_ap, p_=par: e.activation(
                qx[p_][:], xa, AF.Identity,
                bias=nbias[p_][:, 0:1], scale=rstd[p_][:, 0:1]),
              reads=[xk, f"rstd{par}", f"nbias{par}"], writes=[f"qx{par}p0", f"qx{par}p1"])
            transposes(
                lambda i, q_=qx[par]: q_[:, i * P:(i + 1) * P], 8,
                lambda i0, n, b=xqT[par]: b[:, i0:i0 + n, :].rearrange(
                    "p c f -> p (c f)"),
                [f"qx{par}p0", f"qx{par}p1"], [f"xqT{par}"], 6 + par)
            for ch in range(2):
                for k in range(8):
                    S("tensor", lambda e, k_=k, c_=ch, pr=par: e.matmul(
                        pb[c_][:], xqT[pr][:, k_, :],
                        C_wq[:, k_, c_ * 512:(c_ + 1) * 512],
                        start=(k_ == 0), stop=(k_ == 7)),
                      reads=[f"xqT{par}", "C_wq"], writes=[f"pb{ch}"])
                S("scalar", lambda e, c_=ch, pr=par: e.activation(
                    wT1[pr][:, 0:512], pb[c_][:], AF.Relu,
                    scale=np.float32(-1.0)),
                  reads=[f"pb{ch}"], writes=[f"wT1{par}"])
                S("scalar", lambda e, pr=par: e.activation(
                    wA[pr][:, 0:512], wT1[pr][:, 0:512], AF.Exp,
                    scale=np.float32(-1.0)),
                  reads=[f"wT1{par}"], writes=[f"wA{par}"])
                S("vector", lambda e, c_=ch, pr=par: e.scalar_tensor_tensor(
                    qx[pr][:, c_ * 512:(c_ + 1) * 512], pb[c_][:], 0.0,
                    wA[pr][:, 0:512], ALU.max, ALU.add),
                  reads=[f"pb{ch}", f"wA{par}"], writes=[f"qx{par}p{ch}"])
            if c_upto < 2:
                continue
            for half in range(2):
                bki = 6 + half
                bank = pb[bki].ap().bitcast(BF16)
                for j in range(8):
                    h = half * 8 + j
                    S("tensor", lambda e, h_=h, j_=j, b=bank, pr=par: e.transpose(
                        b[0:64, j_ * P:(j_ + 1) * P],
                        qx[pr][:, h_ * 64:(h_ + 1) * 64], ident[:]),
                      reads=[f"qx{par}p0", f"qx{par}p1", "ident"],
                      writes=[f"pb{bki}"])
                S("vector", lambda e, hf=half, b=bank, pr=par: e.tensor_copy(
                    aT2[pr][0:64, hf * 8:(hf + 1) * 8, :].rearrange(
                        "p c f -> p (c f)"),
                    b[0:64, 0:1024]),
                  reads=[f"pb{bki}"], writes=[f"qT{par}"])
            if c_upto < 3:
                continue
            headbank = lambda h: 2 + (h // 7)
            headcol = lambda h: (h % 7) * 65
            for h in range(NH):
                bk, cc = headbank(h), headcol(h)
                S("tensor", lambda e, h_=h, bk_=bk, cc_=cc, pr=par: e.matmul(
                    pb[bk_][:, cc_:cc_ + 65],
                    aT2[pr][0:64, h_, :],
                    C_kvb[0:64, (h_ // 4) * 260 + (h_ % 4) * 65:
                          (h_ // 4) * 260 + (h_ % 4) * 65 + 65],
                    start=True, stop=True),
                  reads=[f"qT{par}", "C_kvb"], writes=[f"pb{bk}"])
            if c_upto < 4:
                continue
            for bk, h0, nh in ((2, 0, 7), (3, 7, 7), (4, 14, 2)):
                S("vector", lambda e, bk_=bk, h0_=h0, nh_=nh, pr=par: e.tensor_scalar_add(
                    rn16[pr][:, h0_:h0_ + nh_],
                    pb[bk_][:, 0:nh_ * 65].rearrange("p (h c) -> p h c", c=65)
                    [:, :, 64:65],
                    float(EPS_ATTN)),
                  reads=[f"pb{bk}"], writes=[f"rn{par}a{bk}"])
            S("vector", lambda e, pr=par: e.reciprocal(rn16[pr][:], rn16[pr][:]),
              reads=[f"rn{par}a2", f"rn{par}a3", f"rn{par}a4"],
              writes=[f"rn{par}"])
            for bk, h0, nh in ((2, 0, 7), (3, 7, 7), (4, 14, 2)):
                S("vector", lambda e, bk_=bk, h0_=h0, nh_=nh, a=C_at[par], pr=par:
                  e.tensor_tensor(
                      a[:, h0_ * 64:(h0_ + nh_) * 64].rearrange(
                          "p (h c) -> p h c", c=64),
                      pb[bk_][:, 0:nh_ * 65].rearrange("p (h c) -> p h c", c=65)
                      [:, :, 0:64],
                      rn16[pr][:, h0_:h0_ + nh_].unsqueeze(2)
                      .broadcast_to([P, nh_, 64]), ALU.mult),
                  reads=[f"pb{bk}", f"rn{par}"], writes=[f"C_at{par}b{bk}"])
            if c_upto < 5:
                continue
            transposes(
                lambda i, a=C_at[par]: a[:, i * P:(i + 1) * P], 8,
                lambda i0, n, b=xqT[par]: b[:, i0:i0 + n, :].rearrange(
                    "p c f -> p (c f)"),
                [f"C_at{par}b2", f"C_at{par}b3", f"C_at{par}b4"],
                [f"xqT{par}"], 6 + par)
            if c_upto < 6:
                continue
            for ch in range(2):
                for k in range(8):
                    S("tensor", lambda e, k_=k, c_=ch, pr=par: e.matmul(
                        pb[c_][:], xqT[pr][:, k_, :],
                        C_wout[:, k_, c_ * 512:(c_ + 1) * 512],
                        start=(k_ == 0), stop=False),
                      reads=[f"xqT{par}", "C_wout"], writes=[f"pb{ch}"])
                S("tensor", lambda e, c_=ch: e.matmul(
                    pb[c_][:], ones_col[:],
                    bout_sb[:, c_ * 512:(c_ + 1) * 512],
                    start=False, stop=True),
                  reads=["ones_col", "bout_sb"], writes=[f"pb{ch}"])
                S("vector", lambda e, c_=ch, t_=t: e.scalar_tensor_tensor(
                    x_res[:, t_, c_ * 512:(c_ + 1) * 512], pb[c_][:], 1.0,
                    x_res[:, t_, c_ * 512:(c_ + 1) * 512], ALU.mult, ALU.add),
                  reads=[f"pb{ch}", f"x:{t}"], writes=[f"x:{t}"])
        s.barrier()

    if stop_after == "C":
        for t in range(T):
            S("gpsimd", lambda e, t_=t: e.dma_start(
                out=y_out.rearrange("(t p) d -> p t d", p=P)[:, t_, :],
                in_=x_res[:, t_, :]),
              reads=[f"x:{t}"], sem="dOUT", amt=16)
        S("sync", lambda e: e.nop(), waits=[("dOUT", s.counts["dOUT"])])
        s.emit_all(block, sems)
        stack.close()
        return nc

    with nc.sbuf_tensor("D_wdT", [P, NKC, DM], BF16) as D_wdT, \
         nc.sbuf_tensor("D_wgu0", [P, 2, 8, 512], BF16) as D_wgu0, \
         nc.sbuf_tensor("D_wgu1", [P, 2, 8, 512], BF16) as D_wgu1:
        ffn_phase("D", D_wdT, [D_wgu0, D_wgu1], out_dma=True)

    S("sync", lambda e: e.nop(), waits=[("dOUT", s.counts["dOUT"])])

    s.emit_all(block, sems)
    stack.close()
    return nc


def kernel(L0_attn_x, L0_attn_Wqkv, L0_attn_Wout, L0_attn_bout,
           L0_ffn_w_gate, L0_ffn_w_up, L0_ffn_w_down):
    from concourse.bass_utils import run_bass_kernel_spmd

    x = np.ascontiguousarray(np.asarray(L0_attn_x, dtype=np.float32))
    B, N, D = x.shape
    T = (N // 2) // P
    wqkv = np.ascontiguousarray(np.asarray(L0_attn_Wqkv, dtype=np.float32))
    wout = np.ascontiguousarray(np.asarray(L0_attn_Wout, dtype=np.float32))
    bout = np.ascontiguousarray(np.asarray(L0_attn_bout, dtype=np.float32)).reshape(1, D)
    wg = np.ascontiguousarray(np.asarray(L0_ffn_w_gate, dtype=np.float32))
    wu = np.ascontiguousarray(np.asarray(L0_ffn_w_up, dtype=np.float32))
    wd = np.ascontiguousarray(np.asarray(L0_ffn_w_down, dtype=np.float32))

    nc = bass.Bass("TRN2", target_bir_lowering=False)
    build(nc, T, 8, [[0, 1], [2, 3], [4, 5], [6, 7]])

    half = N // 2
    in_maps = []
    for c in range(8):
        b, hf = c // 2, c % 2
        sl = np.ascontiguousarray(x[b, hf * half:(hf + 1) * half, :])
        in_maps.append({"x_in": sl, "wqkv": wqkv, "wout": wout, "bout": bout,
                        "wg": wg, "wu": wu, "wd": wd})

    res = run_bass_kernel_spmd(nc, in_maps, core_ids=list(range(8)))
    out = np.empty((B, N, D), dtype=np.float32)
    for c in range(8):
        b, hf = c // 2, c % 2
        out[b, hf * half:(hf + 1) * half, :] = np.asarray(
            res.results[c]["y_out"], dtype=np.float32)
    return out


# version 19 (final)
# speedup vs baseline: 1.0954x; 1.0237x over previous
import numpy as np
from contextlib import ExitStack
import concourse.bass as bass
import concourse.mybir as mybir

F32 = mybir.dt.float32
BF16 = mybir.dt.bfloat16
F16 = mybir.dt.float16
I32 = mybir.dt.int32
AF = mybir.ActivationFunctionType
ALU = mybir.AluOpType
AX = mybir.AxisListType

P = 128
DM = 1024
DI = 2816
NH = 16
DH = 64
NSPAN = 6
SPANS = [(i * 512, min(512, DI - i * 512)) for i in range(NSPAN)]
NKC = DI // P
GROUP = 2
EPS_LN = 1e-5
EPS_ATTN = 1e-6


class Sched:
    ENGS = ("sync", "gpsimd", "scalar", "vector", "tensor")
    ESEM = {"vector": "sDVE", "scalar": "sACT", "gpsimd": "sPOOL",
            "tensor": "sPE", "sync": "sSP"}

    def __init__(self):
        self.steps = {e: [] for e in self.ENGS}
        self.counts = {}
        self.writers = {}
        self.readers = {}

    def step(self, eng, emit, reads=(), writes=(), waits=(), sem="auto", amt=1):
        if sem == "auto":
            sem = self.ESEM[eng]
        wl = set(w for w in waits if w is not None)
        for r in reads:
            t = self.writers.get(r)
            if t is not None:
                wl.add(t)
        for w in writes:
            t = self.writers.get(w)
            if t is not None:
                wl.add(t)
            for t2 in self.readers.get(w, ()):
                wl.add(t2)
        c = self.counts.get(sem, 0) + amt
        self.counts[sem] = c
        tok = (sem, c)
        self.steps[eng].append((emit, sorted(wl), sem, amt))
        for r in reads:
            self.readers.setdefault(r, []).append(tok)
        for w in writes:
            self.writers[w] = tok
            self.readers[w] = []
        return tok

    def barrier(self):
        toks = []
        for e in self.ENGS:
            sem = self.ESEM[e]
            toks.append((sem, self.counts.get(sem, 0)))
        for e in self.ENGS:
            self.step(e, lambda eng: eng.nop(), waits=[t for t in toks])

    def emit_all(self, block, sems):
        for eng_name in self.ENGS:
            steps = self.steps[eng_name]

            def mk(eng_name=eng_name, steps=steps):
                def body(eng):
                    waited = {}
                    own = self.ESEM[eng_name]
                    for (emit, wl, sem, amt) in steps:
                        for (s_, c_) in wl:
                            if s_ == own and eng_name == "tensor":
                                continue
                            if waited.get(s_, -1) < c_:
                                eng.wait_ge(sems[s_], c_)
                                waited[s_] = c_
                        inst = emit(eng)
                        if inst is not None:
                            inst.then_inc(sems[sem], amt)
                return body
            getattr(block, eng_name)(mk())




def build(nc, T, n_cores_collective, replica_groups, dbg=False, stop_after="Z",
          c_upto=9):
    TOK = T * P
    stack = ExitStack()
    en = stack.enter_context
    en(nc.allow_low_precision(
        "f16 tiles only carry fp4-quantization decisions / staircase sums; "
        "validated numerically with 4x margin vs the tolerance"))

    x_in = nc.dram_tensor("x_in", [TOK, DM], F32, kind="ExternalInput").ap()
    wqkv_in = nc.dram_tensor("wqkv", [DM, 3 * DM], F32, kind="ExternalInput").ap()
    wout_in = nc.dram_tensor("wout", [DM, DM], F32, kind="ExternalInput").ap()
    bout_in = nc.dram_tensor("bout", [1, DM], F32, kind="ExternalInput").ap()
    wg_in = nc.dram_tensor("wg", [DI, DM], F32, kind="ExternalInput").ap()
    wu_in = nc.dram_tensor("wu", [DI, DM], F32, kind="ExternalInput").ap()
    wd_in = nc.dram_tensor("wd", [DM, DI], F32, kind="ExternalInput").ap()
    y_out = nc.dram_tensor("y_out", [TOK, DM], F32, kind="ExternalOutput").ap()

    wgu_spill = nc.dram_tensor("wgu_spill", [NSPAN, P, 2, 8, 512], BF16).ap()
    wd_spill = nc.dram_tensor("wd_spill", [P, NKC, DM], BF16).ap()
    kv_src = nc.dram_tensor("kv_src", [64, 1040], F32).ap()
    kv_dst = nc.dram_tensor("kv_dst", [64, 1040], F32).ap()

    s = Sched()
    S = s.step

    x_res = en(nc.sbuf_tensor("x_res", [P, T, DM], BF16))
    ident = en(nc.sbuf_tensor("ident", [P, P], BF16))

    ones_col = en(nc.sbuf_tensor("ones_col", [1, P], BF16))


    nb = 2
    nq = 3
    WQ = GROUP * 512
    wA = [en(nc.sbuf_tensor(f"wA{i}", [P, WQ], F16)) for i in range(nq)]
    wG = [en(nc.sbuf_tensor(f"wG{i}", [P, WQ], F16)) for i in range(nq)]
    wT1 = [en(nc.sbuf_tensor(f"wT1{i}", [P, WQ], F16)) for i in range(nq)]
    wSS = [en(nc.sbuf_tensor(f"wSS{i}", [P, WQ], F16)) for i in range(nq)]
    h_sp = [en(nc.sbuf_tensor(f"h_sp{i}", [P, WQ], F16)) for i in range(nq)]
    wU = h_sp
    g_gm = [en(nc.sbuf_tensor(f"g_gm{i}", [P, 64], F16)) for i in range(nq)]
    g_r = [en(nc.sbuf_tensor(f"g_r{i}", [P, 64], F16)) for i in range(nq)]
    g_sh = [en(nc.sbuf_tensor(f"g_sh{i}", [P, 64], F16)) for i in range(nq)]
    qx = [en(nc.sbuf_tensor(f"qx{i}", [P, DM], BF16)) for i in range(nb)]
    xqT = [en(nc.sbuf_tensor(f"xqT{i}", [P, 8, P], BF16)) for i in range(max(GROUP, nb))]

    hqT = [en(nc.sbuf_tensor(f"hqT{i}", [P, NKC, P], BF16)) for i in range(GROUP)]
    qh_all = en(nc.sbuf_tensor("qh_all", [P, GROUP, DI], BF16))
    lnsum = [en(nc.sbuf_tensor(f"lnsum{i}", [P, 1], F32)) for i in range(nb)]
    lnss = [en(nc.sbuf_tensor(f"lnss{i}", [P, 1], F32)) for i in range(nb)]
    lnm = [en(nc.sbuf_tensor(f"lnm{i}", [P, 1], F32)) for i in range(nb)]
    lnv = [en(nc.sbuf_tensor(f"lnv{i}", [P, 1], F32)) for i in range(nb)]
    rstd = [en(nc.sbuf_tensor(f"rstd{i}", [P, 1], F32)) for i in range(nb)]
    nbias = [en(nc.sbuf_tensor(f"nbias{i}", [P, 1], F32)) for i in range(nb)]



    pb = [en(nc.psum_tensor(f"pb{i}", [P, 512], F32)) for i in range(8)]

    sem_names = ["sDVE", "sACT", "sPOOL", "sPE", "sSP",
                 "dXIN", "dW0", "dW1", "dMISC", "dSPILL", "dOUT", "dKV", "cKV",
                 "dBOUT", "dWKV", "dWQ", "dWOUT", "dKVL", "dWDT", "dSPILL1", "dMISC1", "dWU0", "dWU1"]
    sems = {n: en(nc.semaphore(n)) for n in sem_names}

    block = en(nc.Block())

    def view16(ap, w):
        return ap.rearrange("p (g w) -> p g w", w=16)

    def bc16(small_ap, g16):
        return small_ap.unsqueeze(2).broadcast_to([P, g16, 16])

    def quant_chain(par, width, src_emit_abs, src_emit_sgn, src_reads,
                    q_dst_ap, q_write_res, extra_waits=(), B=None):
        g16 = width // 16
        if B is None:
            A_, G_, U_, T1, SS = wA[par], wG[par], wU[par], wT1[par], wSS[par]
            T2 = A_
            GM, R_, SH = g_gm[par], g_r[par], g_sh[par]
            ra = rt2 = f"wA{par}"
            rg, ru, rt1, rss = f"wG{par}", f"h_sp{par}", f"wT1{par}", f"wSS{par}"
            rgm, rr, rsh = f"g_gm{par}", f"g_r{par}", f"g_sh{par}"
        else:
            bufs, kp = B
            A_, G_, U_, T1, T2, SS, GM, SC_, R_, SH = (b[par] for b in bufs)
            ra, rg, ru, rt1, rt2, rss = (f"{kp}a{par}", f"{kp}g{par}",
                                         f"{kp}u{par}", f"{kp}t1{par}",
                                         f"{kp}t2{par}", f"{kp}ss{par}")
            rgm, rr, rsh = f"{kp}gm{par}", f"{kp}r{par}", f"{kp}sh{par}"

        S("scalar", lambda e: src_emit_abs(e, A_[:, :width]),
          reads=src_reads, writes=[ra], waits=extra_waits)
        S("scalar", lambda e: src_emit_sgn(e, G_[:, :width]),
          reads=src_reads, writes=[rg], waits=extra_waits)
        S("vector", lambda e: e.tensor_reduce(
            GM[:, :g16], view16(A_[:, :width], width), AX.X, ALU.max),
          reads=[ra], writes=[rgm])
        S("vector", lambda e: e.tensor_scalar(
            SH[:, :g16], GM[:, :g16], 1.0 / 12.0, 5e-13, ALU.mult, ALU.max),
          reads=[rgm], writes=[rsh])
        S("vector", lambda e: e.reciprocal(R_[:, :g16], GM[:, :g16]),
          reads=[rgm], writes=[rr])
        S("vector", lambda e: e.tensor_scalar(
            R_[:, :g16], R_[:, :g16], 6.0, 1e12, ALU.mult, ALU.min),
          reads=[rr], writes=[rr])
        S("vector", lambda e: e.tensor_tensor(
            view16(U_[:, :width], width), view16(A_[:, :width], width),
            bc16(R_[:, :g16], g16), ALU.mult),
          reads=[ra, rr], writes=[ru])
        S("gpsimd", lambda e: e.tensor_tensor(
            view16(SS[:, :width], width), view16(G_[:, :width], width),
            bc16(SH[:, :g16], g16), ALU.mult),
          reads=[rg, rsh], writes=[rss])
        S("vector", lambda e: e.tensor_scalar(
            T1[:, :width], U_[:, :width], 5.0, 2.0, ALU.is_gt, ALU.mult),
          reads=[ru], writes=[rt1])
        S("vector", lambda e: e.scalar_tensor_tensor(
            T2[:, :width], U_[:, :width], 3.5, T1[:, :width], ALU.is_gt, ALU.add),
          reads=[ru, rt1], writes=[rt2])
        S("vector", lambda e: e.scalar_tensor_tensor(
            T1[:, :width], U_[:, :width], 2.5, T2[:, :width], ALU.is_gt, ALU.add),
          reads=[ru, rt2], writes=[rt1])
        S("vector", lambda e: e.tensor_scalar_mul(
            T2[:, :width], T1[:, :width], 2.0),
          reads=[rt1], writes=[rt2])
        S("vector", lambda e: e.scalar_tensor_tensor(
            T1[:, :width], U_[:, :width], 1.75, T2[:, :width], ALU.is_gt, ALU.add),
          reads=[ru, rt2], writes=[rt1])
        S("vector", lambda e: e.scalar_tensor_tensor(
            T2[:, :width], U_[:, :width], 1.25, T1[:, :width], ALU.is_gt, ALU.add),
          reads=[ru, rt1], writes=[rt2])
        S("vector", lambda e: e.scalar_tensor_tensor(
            T1[:, :width], U_[:, :width], 0.75, T2[:, :width], ALU.is_gt, ALU.add),
          reads=[ru, rt2], writes=[rt1])
        S("vector", lambda e: e.scalar_tensor_tensor(
            T2[:, :width], U_[:, :width], 0.25, T1[:, :width], ALU.is_gt, ALU.add),
          reads=[ru, rt1], writes=[rt2])
        tq = S("vector", lambda e: e.tensor_mul(
            q_dst_ap, T2[:, :width], SS[:, :width]),
          reads=[rt2, rss], writes=[q_write_res])
        return tq

    def transposes(src_fn, nchunks, dst_fn, src_reads, dst_writes, bank_idx):
        i = 0
        while i < nchunks:
            n = min(8, nchunks - i)
            bank = pb[bank_idx].ap().bitcast(BF16)
            for j in range(n):
                S("tensor", lambda e, jj=i + j, oj=j, b=bank: e.transpose(
                    b[:, oj * P:(oj + 1) * P], src_fn(jj), ident[:]),
                  reads=src_reads + ["ident"], writes=[f"pb{bank_idx}"])
            S("scalar", lambda e, ii=i, n_=n, b=bank: e.copy(
                dst_fn(ii, n_), b[:, :n_ * P]),
              reads=[f"pb{bank_idx}"], writes=dst_writes)
            i += n

    def ln_stats(par, x_ap, x_res_key):
        S("scalar", lambda e: e.activation(
            qx[par][:], x_ap, AF.Identity, accum_out=lnsum[par][:]),
          reads=[x_res_key],
          writes=[f"lnsum{par}", f"qx{par}p0", f"qx{par}p1"])
        S("scalar", lambda e: e.activation(
            qx[par][:], x_ap, AF.Square, accum_out=lnss[par][:]),
          reads=[x_res_key],
          writes=[f"lnss{par}", f"qx{par}p0", f"qx{par}p1"])
        S("vector", lambda e: e.tensor_scalar_mul(
            lnm[par][:], lnsum[par][:], 1.0 / DM),
          reads=[f"lnsum{par}"], writes=[f"lnm{par}"])
        S("vector", lambda e: e.tensor_scalar_mul(
            lnv[par][:], lnss[par][:], 1.0 / DM),
          reads=[f"lnss{par}"], writes=[f"lnv{par}"])
        S("vector", lambda e: e.scalar_tensor_tensor(
            lnv[par][:], lnm[par][:], lnm[par][:, 0:1], lnv[par][:],
            ALU.mult, ALU.subtract),
          reads=[f"lnm{par}", f"lnv{par}"], writes=[f"lnv{par}"])
        S("scalar", lambda e: e.activation(
            rstd[par][:], lnv[par][:], AF.Ln, bias=np.float32(EPS_LN), scale=np.float32(-1.0)),
          reads=[f"lnv{par}"], writes=[f"rstd{par}"])
        S("scalar", lambda e: e.activation(
            rstd[par][:], rstd[par][:], AF.Exp, scale=np.float32(-0.5)),
          reads=[f"rstd{par}"], writes=[f"rstd{par}"])
        S("vector", lambda e: e.scalar_tensor_tensor(
            nbias[par][:], lnm[par][:], -1.0, rstd[par][:], ALU.mult, ALU.mult),
          reads=[f"lnm{par}", f"rstd{par}"], writes=[f"nbias{par}"])

    sustack = ExitStack()
    identi = sustack.enter_context(nc.sbuf_tensor("identi", [P, P], I32))
    wstage = [sustack.enter_context(nc.sbuf_tensor(f"wstage{i}", [P, DM], F32))
              for i in range(2)]

    x_view = x_in.rearrange("(t p) d -> p t d", p=P)
    nxd = 4 if T >= 4 else 1
    step_t = (T + nxd - 1) // nxd
    for i in range(0, T, step_t):
        n = min(step_t, T - i)
        S("gpsimd", lambda e, i_=i, n_=n: e.dma_start(
            out=x_res[:, i_:i_ + n_, :], in_=x_view[:, i_:i_ + n_, :]),
          writes=[f"x:{t}" for t in range(i, i + n)], sem="dXIN", amt=16)
    tok_x_all = ("dXIN", s.counts["dXIN"])
    S("sync", lambda e: e.nop(), waits=[tok_x_all],
      writes=[f"x:{t}" for t in range(T)])

    S("gpsimd", lambda e: e.iota(identi[:], pattern=[[1, P]], base=0,
                                 channel_multiplier=-1), writes=["identi"])
    S("gpsimd", lambda e: e.tensor_scalar(
        ident[:], identi[:], 0.0, None, ALU.is_equal),
      reads=["identi"], writes=["ident"])
    S("vector", lambda e: e.memset(ones_col[:], 1.0), writes=["ones_col"])

    spill_tok = {}
    wgv = wg_in.rearrange("(c p) d -> c p d", p=P)
    wuv = wu_in.rearrange("(c p) d -> c p d", p=P)
    for c in range(NKC):
        for mi, wv in ((0, wgv[c]), (1, wuv[c])):
            par = (2 * c + mi) % nq
            wpar = (2 * c + mi) % 2
            ws = wstage[wpar]
            wsk = f"wstage{wpar}"
            S("sync", lambda e, wv_=wv, w_=ws: e.dma_start(out=w_[:], in_=wv_),
              writes=[wsk], sem=("dMISC" if wpar == 0 else "dMISC1"), amt=16)
            qpar = wpar
            quant_chain(
                par, DM,
                lambda e, d, w_=ws: e.activation(d, w_[:], AF.Abs),
                lambda e, d, w_=ws: e.activation(d, w_[:], AF.Sign),
                [wsk],
                qx[qpar][:], f"qx{qpar}p0")
            transposes(
                lambda i, q_=qx[qpar]: q_[:, i * P:(i + 1) * P], 8,
                lambda i0_, n, b=xqT[qpar]: b[:, i0_:i0_ + n, :].rearrange(
                    "p c f -> p (c f)"),
                [f"qx{qpar}p0"], [f"xqT{qpar}"], 6 + qpar)
            sp_, off_ = (c * P) // 512, (c * P) % 512
            spill_tok[(mi, c)] = S(
                "gpsimd", lambda e, s_=sp_, o_=off_, b=xqT[qpar], si=mi:
                e.dma_start(out=wgu_spill[s_, :, si, :, o_:o_ + P], in_=b[:]),
                reads=[f"xqT{qpar}"],
                sem=("dSPILL" if qpar == 0 else "dSPILL1"), amt=16)

    with ExitStack() as wstack:
        wen = wstack.enter_context
        w32 = tuple(
            [wen(nc.sbuf_tensor(f"w32{nm}{i}", [P, w_], F32)) for i in range(nb)]
            for nm, w_ in (("a", 1024), ("g", 1024), ("u", 1024), ("t1", 1024),
                           ("t2", 1024), ("ss", 1024), ("gm", 64), ("sc", 64),
                           ("r", 64), ("sh", 64)))
        B32 = (w32, "w32")
        wdv = wd_in.rearrange("(c p) d -> c p d", p=P)
        for c in range(8):
            for piece in range(3):
                poff = piece * 1024
                plen = min(1024, DI - poff)
                par = (3 * c + piece) % 2
                ws = wstage[par]
                wsk = f"wstage{par}"
                S("sync", lambda e, wv_=wdv[c], o=poff, L=plen, w_=ws:
                  e.dma_start(out=w_[:, 0:L], in_=wv_[:, o:o + L]),
                  writes=[wsk], sem=("dMISC" if par == 0 else "dMISC1"), amt=16)
                quant_chain(
                    par, plen,
                    lambda e, d, L=plen, w_=ws: e.activation(
                        d, w_[:, 0:L], AF.Abs),
                    lambda e, d, L=plen, w_=ws: e.activation(
                        d, w_[:, 0:L], AF.Sign),
                    [wsk],
                    qx[par][:, :plen], f"qx{par}p0", B=B32)
                nch = plen // P
                transposes(
                    lambda i, q_=qx[par]: q_[:, i * P:(i + 1) * P], nch,
                    lambda i0_, n, b=xqT[par]: b[:, i0_:i0_ + n, :].rearrange(
                        "p c f -> p (c f)"),
                    [f"qx{par}p0"], [f"xqT{par}"], 6 + par)
                spill_tok[(2, 3 * c + piece)] = S(
                    "gpsimd", lambda e, b=xqT[par], k0=poff // P, n=nch, cc=c:
                    e.dma_start(out=wd_spill[:, k0:k0 + n, cc * P:(cc + 1) * P],
                                in_=b[:, 0:n, :]),
                    reads=[f"xqT{par}"],
                    sem=("dSPILL" if par == 0 else "dSPILL1"), amt=16)

    tok_spill_all = ("dSPILL", s.counts["dSPILL"])
    tok_spill_all2 = ("dSPILL1", s.counts["dSPILL1"])
    s.barrier()
    sustack.close()

    def ffn_phase(phase, wdT, wgu, out_dma=False):
        wdT_key = f"{phase}:wdT"
        S("sync", lambda e: e.dma_start(out=wdT[:], in_=wd_spill),
          waits=[spill_tok[(2, i_)] for i_ in range(24)], writes=[wdT_key],
          sem="dWDT", amt=16)

        ngroups = (T + GROUP - 1) // GROUP
        span_loads = {}
        for g in range(ngroups):
            tiles = list(range(g * GROUP, min(T, (g + 1) * GROUP)))
            for t in tiles:
                par = t % 2
                gsl = t % GROUP
                xk = f"x:{t}"
                x_ap = x_res[:, t, :]
                ln_stats(par, x_ap, xk)
                quant_chain(
                    t % nq, DM,
                    lambda e, d, xa=x_ap, p_=par: e.activation(
                        d, xa, AF.Abs,
                        bias=nbias[p_][:, 0:1], scale=rstd[p_][:, 0:1]),
                    lambda e, d, xa=x_ap, p_=par: e.activation(
                        d, xa, AF.Sign,
                        bias=nbias[p_][:, 0:1], scale=rstd[p_][:, 0:1]),
                    [xk, f"rstd{par}", f"nbias{par}"],
                    qx[par][:], f"qx{par}p0")
                transposes(
                    lambda i, q_=qx[par]: q_[:, i * P:(i + 1) * P], 8,
                    lambda i0, n, b=xqT[gsl]: b[:, i0:i0 + n, :].rearrange(
                        "p c f -> p (c f)"),
                    [f"qx{par}p0"], [f"xqT{gsl}"], 6 + par)

            for sp in range(NSPAN):
                soff, slen = SPANS[sp]
                buf = sp % 2
                wk = f"{phase}:wgu{buf}"
                dsem = "dW0" if buf == 0 else "dW1"
                wkg, wku = wk + "g", wk + "u"
                dsem_u = "dWU0" if buf == 0 else "dWU1"
                cw = range(sp * 4, min(NKC, sp * 4 + 4))
                S("sync", lambda e, sp_=sp, b=wgu[buf], L=slen: e.dma_start(
                    out=b[:, 0, :, 0:L], in_=wgu_spill[sp_][:, 0, :, 0:L]),
                  waits=[spill_tok[(0, c_)] for c_ in cw], writes=[wkg],
                  sem=dsem, amt=16)
                S("sync", lambda e, sp_=sp, b=wgu[buf], L=slen: e.dma_start(
                    out=b[:, 1, :, 0:L], in_=wgu_spill[sp_][:, 1, :, 0:L]),
                  waits=[spill_tok[(1, c_)] for c_ in cw], writes=[wku],
                  sem=dsem_u, amt=16)
                par2 = sp % nq
                ntl = len(tiles)
                for t in tiles:
                    gsl = t % GROUP
                    it = sp * GROUP + gsl
                    pp2 = it % 2
                    bg, bu = (0, 1) if pp2 == 0 else (2, 3)
                    for k in range(8):
                        S("tensor", lambda e, k_=k, b=wgu[buf], pg=pb[bg],
                          xt=xqT[gsl], L=slen: e.matmul(
                              pg[:, :L], xt[:, k_, :], b[:, 0, k_, 0:L],
                              start=(k_ == 0), stop=(k_ == 7)),
                          reads=[f"xqT{gsl}", wkg], writes=[f"pb{bg}"])
                    for k in range(8):
                        S("tensor", lambda e, k_=k, b=wgu[buf], pu=pb[bu],
                          xt=xqT[gsl], L=slen: e.matmul(
                              pu[:, :L], xt[:, k_, :], b[:, 1, k_, 0:L],
                              start=(k_ == 0), stop=(k_ == 7)),
                          reads=[f"xqT{gsl}", wku], writes=[f"pb{bu}"])
                    S("scalar", lambda e, pg=pb[bg], L=slen, p2=par2, g_=gsl:
                      e.activation(
                          wA[p2][:, g_ * L:(g_ + 1) * L], pg[:, :L], AF.Silu),
                      reads=[f"pb{bg}"], writes=[f"wA{par2}"])
                    S("vector", lambda e, pu=pb[bu], L=slen, p2=par2, g_=gsl:
                      e.tensor_mul(
                          h_sp[p2][:, g_ * L:(g_ + 1) * L],
                          wA[p2][:, g_ * L:(g_ + 1) * L], pu[:, :L]),
                      reads=[f"wA{par2}", f"pb{bu}"], writes=[f"h_sp{par2}"])
                wqw = ntl * slen
                quant_chain(
                    par2, wqw,
                    lambda e, d, p2=par2, W_=wqw: e.activation(
                        d, h_sp[p2][:, :W_], AF.Abs),
                    lambda e, d, p2=par2, W_=wqw: e.activation(
                        d, h_sp[p2][:, :W_], AF.Sign),
                    [f"h_sp{par2}"],
                    qh_all[:, 0:ntl, soff:soff + slen],
                    f"qhs{sp}g{g}")
            for t in tiles:
                gsl = t % GROUP
                for sp in range(NSPAN):
                    soff, slen = SPANS[sp]
                    nch = slen // P
                    transposes(
                        lambda i, g_=gsl, o=soff: qh_all[:, g_, o + i * P:
                                                         o + (i + 1) * P],
                        nch,
                        lambda i0, n, b=hqT[gsl], k0=soff // P:
                            b[:, k0 + i0:k0 + i0 + n, :].rearrange(
                                "p c f -> p (c f)"),
                        [f"qhs{sp}g{g}"], [f"hqT{gsl}s{sp}"], 6 + (sp % 2))
            for t in tiles:
                gsl = t % GROUP
                hk = [f"hqT{gsl}s{sp}" for sp in range(NSPAN)]
                for k in range(NKC):
                    S("tensor", lambda e, k_=k, b=hqT[gsl]: e.matmul(
                        pb[4][:], b[:, k_, :], wdT[:, k_, 0:512],
                        start=(k_ == 0), stop=(k_ == NKC - 1)),
                      reads=hk + [wdT_key], writes=["pb4"])
                    S("tensor", lambda e, k_=k, b=hqT[gsl]: e.matmul(
                        pb[5][:], b[:, k_, :], wdT[:, k_, 512:1024],
                        start=(k_ == 0), stop=(k_ == NKC - 1)),
                      reads=hk + [wdT_key], writes=["pb5"])
                for half in range(2):
                    bk = 4 + half
                    S("vector", lambda e, t_=t, h_=half, b=pb[bk]: e.scalar_tensor_tensor(
                        x_res[:, t_, h_ * 512:(h_ + 1) * 512], b[:], 0.5,
                        x_res[:, t_, h_ * 512:(h_ + 1) * 512], ALU.mult, ALU.add),
                      reads=[f"pb{bk}", f"x:{t}"], writes=[f"x:{t}"])
                if out_dma:
                    S("gpsimd", lambda e, t_=t: e.dma_start(
                        out=y_out.rearrange("(t p) d -> p t d", p=P)[:, t_, :],
                        in_=x_res[:, t_, :]),
                      reads=[f"x:{t}"], sem="dOUT", amt=16)

    with nc.sbuf_tensor("A_wdT", [P, NKC, DM], BF16) as A_wdT, \
         nc.sbuf_tensor("A_wgu0", [P, 2, 8, 512], BF16) as A_wgu0, \
         nc.sbuf_tensor("A_wgu1", [P, 2, 8, 512], BF16) as A_wgu1:
        ffn_phase("A", A_wdT, [A_wgu0, A_wgu1], out_dma=(stop_after == "A"))
        s.barrier()

    if stop_after == "A":
        S("sync", lambda e: e.nop(), waits=[("dOUT", s.counts.get("dOUT", 0))])
        s.emit_all(block, sems)
        stack.close()
        return nc

    with nc.sbuf_tensor("B_wkv", [P, 8, 2048], BF16) as B_wkv, \
         nc.sbuf_tensor("B_kk0", [P, DM], BF16) as B_kk0, \
         nc.sbuf_tensor("B_kk1", [P, DM], BF16) as B_kk1, \
         nc.sbuf_tensor("B_vx0", [P, NH * 65], BF16) as B_vx0, \
         nc.sbuf_tensor("B_vx1", [P, NH * 65], BF16) as B_vx1, \
         nc.sbuf_tensor("B_kvf", [64, 1040], F32) as B_kvf, \
         nc.sbuf_tensor("zeros_row", [1, 512], BF16) as zeros_row:
        B_kk = [B_kk0, B_kk1]
        B_vx = [B_vx0, B_vx1]
        wkv_view = wqkv_in.rearrange("(c p) n -> p c n", p=P)
        S("gpsimd", lambda e: e.dma_start(
            out=B_wkv[:], in_=wkv_view[:, :, DM:3 * DM]),
          writes=["B_wkv"], sem="dWKV", amt=16)
        for i in range(nb):
            S("vector", lambda e, b=B_vx[i]: e.memset(
                b[:].rearrange("p (h c) -> p h c", c=65)[:, :, 64:65], 1.0),
              writes=[f"B_vxones{i}"])

        S("vector", lambda e: e.memset(zeros_row[:], 0.0), writes=["zeros_row"])
        for bk in range(4):
            S("tensor", lambda e, b=bk: e.matmul(
                pb[b][0:64, 0:260], zeros_row[0:1, 0:64], zeros_row[0:1, 0:260],
                start=True, stop=False),
              reads=["zeros_row"], writes=[f"pb{bk}"])

        for t in range(T):
            par = t % 2
            xk = f"x:{t}"
            x_ap = x_res[:, t, :]
            ln_stats(par, x_ap, xk)
            S("scalar", lambda e, xa=x_ap, p_=par: e.activation(
                qx[p_][:], xa, AF.Identity,
                bias=nbias[p_][:, 0:1], scale=rstd[p_][:, 0:1]),
              reads=[xk, f"rstd{par}", f"nbias{par}"], writes=[f"qx{par}p0", f"qx{par}p1"])
            transposes(
                lambda i, q_=qx[par]: q_[:, i * P:(i + 1) * P], 8,
                lambda i0, n, b=xqT[par]: b[:, i0:i0 + n, :].rearrange(
                    "p c f -> p (c f)"),
                [f"qx{par}p0", f"qx{par}p1"], [f"xqT{par}"], 6 + par)
            for ch in range(4):
                bk = 4 + (ch % 2)
                for k in range(8):
                    S("tensor", lambda e, k_=k, c_=ch, b=B_wkv, p_=pb[bk],
                      xt=xqT[par]: e.matmul(
                          p_[:], xt[:, k_, :], b[:, k_, c_ * 512:(c_ + 1) * 512],
                          start=(k_ == 0), stop=(k_ == 7)),
                      reads=[f"xqT{par}", "B_wkv"], writes=[f"pb{bk}"])
                if ch < 2:
                    S("scalar", lambda e, p_=pb[bk], pr=par: e.activation(
                        wT1[pr][:, 0:512], p_[:], AF.Relu,
                        scale=np.float32(-1.0)),
                      reads=[f"pb{bk}"], writes=[f"wT1{par}"])
                    S("scalar", lambda e, pr=par: e.activation(
                        wA[pr][:, 0:512], wT1[pr][:, 0:512], AF.Exp,
                        scale=np.float32(-1.0)),
                      reads=[f"wT1{par}"], writes=[f"wA{par}"])
                    S("vector", lambda e, p_=pb[bk], c_=ch, kk=B_kk[par], pr=par:
                      e.scalar_tensor_tensor(
                          kk[:, c_ * 512:(c_ + 1) * 512], p_[:], 0.0,
                          wA[pr][:, 0:512], ALU.max, ALU.add),
                      reads=[f"pb{bk}", f"wA{par}"], writes=[f"B_kk{par}c{ch}"])
                else:
                    h0 = (ch - 2) * 8
                    S("scalar", lambda e, p_=pb[bk], h0_=h0, vx=B_vx[par]:
                      e.activation(
                          vx[:].rearrange("p (h c) -> p h c", c=65)
                          [:, h0_:h0_ + 8, 0:64],
                          p_[:], AF.Copy),
                      reads=[f"pb{bk}"], writes=[f"B_vx{par}h{h0}"])
            for h in range(NH):
                bk = h // 4
                cc = 65 * (h % 4)
                S("tensor", lambda e, h_=h, bk_=bk, cc_=cc,
                  kk=B_kk[par], vx=B_vx[par],
                  sp_=(t == T - 1 and h % 4 == 3):
                  e.matmul(
                      pb[bk_][0:64, cc_:cc_ + 65],
                      kk[:, h_ * 64:(h_ + 1) * 64],
                      vx[:, h_ * 65:(h_ + 1) * 65],
                      start=False, stop=sp_),
                  reads=[f"B_kk{par}c0", f"B_kk{par}c1",
                         f"B_vx{par}h0", f"B_vx{par}h8", f"B_vxones{par}"],
                  writes=[f"pb{bk}"] if t == T - 1 else [],
                  waits=[],
                  )
        for bk in range(4):
            S("vector", lambda e, b=bk: e.tensor_copy(
                B_kvf[0:64, b * 260:(b + 1) * 260], pb[b][0:64, 0:260]),
              reads=[f"pb{bk}"], writes=["B_kvf"])
        S("gpsimd", lambda e: e.dma_start(out=kv_src, in_=B_kvf[:]),
          reads=["B_kvf"], sem="dKV", amt=16)
        S("gpsimd", lambda e: e.collective_compute(
            "AllReduce", ALU.add, replica_groups=replica_groups,
            ins=[kv_src.opt()], outs=[kv_dst.opt()]),
          waits=[("dKV", s.counts["dKV"])], sem="cKV", amt=1)
        s.barrier()

    if stop_after == "B":
        for t in range(T):
            S("gpsimd", lambda e, t_=t: e.dma_start(
                out=y_out.rearrange("(t p) d -> p t d", p=P)[:, t_, :],
                in_=x_res[:, t_, :]),
              reads=[f"x:{t}"], sem="dOUT", amt=16)
        S("sync", lambda e: e.nop(), waits=[("dOUT", s.counts["dOUT"])])
        s.emit_all(block, sems)
        stack.close()
        return nc

    with nc.sbuf_tensor("C_wq", [P, 8, DM], BF16) as C_wq, \
         nc.sbuf_tensor("C_wout", [P, 8, DM], BF16) as C_wout, \
         nc.sbuf_tensor("C_kv", [64, 1040], F32) as C_kv, \
         nc.sbuf_tensor("C_kvb", [64, 1040], BF16) as C_kvb, \
         nc.sbuf_tensor("C_at0", [P, DM], BF16) as C_at0, \
         nc.sbuf_tensor("C_at1", [P, DM], BF16) as C_at1, \
         nc.sbuf_tensor("aT2_0", [64, NH, P], BF16) as aT2_0, \
         nc.sbuf_tensor("aT2_1", [64, NH, P], BF16) as aT2_1, \
         nc.sbuf_tensor("rn16_0", [P, NH], F32) as rn16_0, \
         nc.sbuf_tensor("rn16_1", [P, NH], F32) as rn16_1, \
         nc.sbuf_tensor("bout_sb", [1, DM], BF16) as bout_sb:
        C_at = [C_at0, C_at1]
        aT2 = [aT2_0, aT2_1]
        rn16 = [rn16_0, rn16_1]
        S("gpsimd", lambda e: e.dma_start(out=bout_sb[:], in_=bout_in),
          writes=["bout_sb"], sem="dBOUT", amt=16)
        wq_view = wqkv_in.rearrange("(c p) n -> p c n", p=P)
        S("gpsimd", lambda e: e.dma_start(out=C_wq[:], in_=wq_view[:, :, 0:DM]),
          writes=["C_wq"], sem="dWQ", amt=16)
        wo_view = wout_in.rearrange("(c p) n -> p c n", p=P)
        S("gpsimd", lambda e: e.dma_start(out=C_wout[:], in_=wo_view),
          writes=["C_wout"], sem="dWOUT", amt=16)
        tok_ckv = ("cKV", s.counts["cKV"])
        S("sync", lambda e: e.dma_start(out=C_kv[:], in_=kv_dst),
          waits=[tok_ckv], writes=["C_kv"], sem="dKVL", amt=16)
        S("vector", lambda e: e.tensor_copy(C_kvb[:], C_kv[:]),
          reads=["C_kv"], writes=["C_kvb"])

        for t in range(T):
            par = t % 2
            xk = f"x:{t}"
            x_ap = x_res[:, t, :]
            ln_stats(par, x_ap, xk)
            S("scalar", lambda e, xa=x_ap, p_=par: e.activation(
                qx[p_][:], xa, AF.Identity,
                bias=nbias[p_][:, 0:1], scale=rstd[p_][:, 0:1]),
              reads=[xk, f"rstd{par}", f"nbias{par}"], writes=[f"qx{par}p0", f"qx{par}p1"])
            transposes(
                lambda i, q_=qx[par]: q_[:, i * P:(i + 1) * P], 8,
                lambda i0, n, b=xqT[par]: b[:, i0:i0 + n, :].rearrange(
                    "p c f -> p (c f)"),
                [f"qx{par}p0", f"qx{par}p1"], [f"xqT{par}"], 6 + par)
            for ch in range(2):
                for k in range(8):
                    S("tensor", lambda e, k_=k, c_=ch, pr=par: e.matmul(
                        pb[c_][:], xqT[pr][:, k_, :],
                        C_wq[:, k_, c_ * 512:(c_ + 1) * 512],
                        start=(k_ == 0), stop=(k_ == 7)),
                      reads=[f"xqT{par}", "C_wq"], writes=[f"pb{ch}"])
                S("scalar", lambda e, c_=ch, pr=par: e.activation(
                    wT1[pr][:, 0:512], pb[c_][:], AF.Relu,
                    scale=np.float32(-1.0)),
                  reads=[f"pb{ch}"], writes=[f"wT1{par}"])
                S("scalar", lambda e, pr=par: e.activation(
                    wA[pr][:, 0:512], wT1[pr][:, 0:512], AF.Exp,
                    scale=np.float32(-1.0)),
                  reads=[f"wT1{par}"], writes=[f"wA{par}"])
                S("vector", lambda e, c_=ch, pr=par: e.scalar_tensor_tensor(
                    qx[pr][:, c_ * 512:(c_ + 1) * 512], pb[c_][:], 0.0,
                    wA[pr][:, 0:512], ALU.max, ALU.add),
                  reads=[f"pb{ch}", f"wA{par}"], writes=[f"qx{par}p{ch}"])
            if c_upto < 2:
                continue
            for half in range(2):
                bki = 6 + half
                bank = pb[bki].ap().bitcast(BF16)
                for j in range(8):
                    h = half * 8 + j
                    S("tensor", lambda e, h_=h, j_=j, b=bank, pr=par: e.transpose(
                        b[0:64, j_ * P:(j_ + 1) * P],
                        qx[pr][:, h_ * 64:(h_ + 1) * 64], ident[:]),
                      reads=[f"qx{par}p0", f"qx{par}p1", "ident"],
                      writes=[f"pb{bki}"])
                S("vector", lambda e, hf=half, b=bank, pr=par: e.tensor_copy(
                    aT2[pr][0:64, hf * 8:(hf + 1) * 8, :].rearrange(
                        "p c f -> p (c f)"),
                    b[0:64, 0:1024]),
                  reads=[f"pb{bki}"], writes=[f"qT{par}"])
            if c_upto < 3:
                continue
            headbank = lambda h: 2 + (h // 7)
            headcol = lambda h: (h % 7) * 65
            for h in range(NH):
                bk, cc = headbank(h), headcol(h)
                S("tensor", lambda e, h_=h, bk_=bk, cc_=cc, pr=par: e.matmul(
                    pb[bk_][:, cc_:cc_ + 65],
                    aT2[pr][0:64, h_, :],
                    C_kvb[0:64, (h_ // 4) * 260 + (h_ % 4) * 65:
                          (h_ // 4) * 260 + (h_ % 4) * 65 + 65],
                    start=True, stop=True),
                  reads=[f"qT{par}", "C_kvb"], writes=[f"pb{bk}"])
            if c_upto < 4:
                continue
            for bk, h0, nh in ((2, 0, 7), (3, 7, 7), (4, 14, 2)):
                S("vector", lambda e, bk_=bk, h0_=h0, nh_=nh, pr=par: e.tensor_scalar_add(
                    rn16[pr][:, h0_:h0_ + nh_],
                    pb[bk_][:, 0:nh_ * 65].rearrange("p (h c) -> p h c", c=65)
                    [:, :, 64:65],
                    float(EPS_ATTN)),
                  reads=[f"pb{bk}"], writes=[f"rn{par}a{bk}"])
            S("vector", lambda e, pr=par: e.reciprocal(rn16[pr][:], rn16[pr][:]),
              reads=[f"rn{par}a2", f"rn{par}a3", f"rn{par}a4"],
              writes=[f"rn{par}"])
            for bk, h0, nh in ((2, 0, 7), (3, 7, 7), (4, 14, 2)):
                S("vector", lambda e, bk_=bk, h0_=h0, nh_=nh, a=C_at[par], pr=par:
                  e.tensor_tensor(
                      a[:, h0_ * 64:(h0_ + nh_) * 64].rearrange(
                          "p (h c) -> p h c", c=64),
                      pb[bk_][:, 0:nh_ * 65].rearrange("p (h c) -> p h c", c=65)
                      [:, :, 0:64],
                      rn16[pr][:, h0_:h0_ + nh_].unsqueeze(2)
                      .broadcast_to([P, nh_, 64]), ALU.mult),
                  reads=[f"pb{bk}", f"rn{par}"], writes=[f"C_at{par}b{bk}"])
            if c_upto < 5:
                continue
            transposes(
                lambda i, a=C_at[par]: a[:, i * P:(i + 1) * P], 8,
                lambda i0, n, b=xqT[par]: b[:, i0:i0 + n, :].rearrange(
                    "p c f -> p (c f)"),
                [f"C_at{par}b2", f"C_at{par}b3", f"C_at{par}b4"],
                [f"xqT{par}"], 6 + par)
            if c_upto < 6:
                continue
            for ch in range(2):
                for k in range(8):
                    S("tensor", lambda e, k_=k, c_=ch, pr=par: e.matmul(
                        pb[c_][:], xqT[pr][:, k_, :],
                        C_wout[:, k_, c_ * 512:(c_ + 1) * 512],
                        start=(k_ == 0), stop=False),
                      reads=[f"xqT{par}", "C_wout"], writes=[f"pb{ch}"])
                S("tensor", lambda e, c_=ch: e.matmul(
                    pb[c_][:], ones_col[:],
                    bout_sb[:, c_ * 512:(c_ + 1) * 512],
                    start=False, stop=True),
                  reads=["ones_col", "bout_sb"], writes=[f"pb{ch}"])
                S("vector", lambda e, c_=ch, t_=t: e.scalar_tensor_tensor(
                    x_res[:, t_, c_ * 512:(c_ + 1) * 512], pb[c_][:], 1.0,
                    x_res[:, t_, c_ * 512:(c_ + 1) * 512], ALU.mult, ALU.add),
                  reads=[f"pb{ch}", f"x:{t}"], writes=[f"x:{t}"])
        s.barrier()

    if stop_after == "C":
        for t in range(T):
            S("gpsimd", lambda e, t_=t: e.dma_start(
                out=y_out.rearrange("(t p) d -> p t d", p=P)[:, t_, :],
                in_=x_res[:, t_, :]),
              reads=[f"x:{t}"], sem="dOUT", amt=16)
        S("sync", lambda e: e.nop(), waits=[("dOUT", s.counts["dOUT"])])
        s.emit_all(block, sems)
        stack.close()
        return nc

    with nc.sbuf_tensor("D_wdT", [P, NKC, DM], BF16) as D_wdT, \
         nc.sbuf_tensor("D_wgu0", [P, 2, 8, 512], BF16) as D_wgu0, \
         nc.sbuf_tensor("D_wgu1", [P, 2, 8, 512], BF16) as D_wgu1:
        ffn_phase("D", D_wdT, [D_wgu0, D_wgu1], out_dma=True)

    S("sync", lambda e: e.nop(), waits=[("dOUT", s.counts["dOUT"])])

    s.emit_all(block, sems)
    stack.close()
    return nc


def kernel(L0_attn_x, L0_attn_Wqkv, L0_attn_Wout, L0_attn_bout,
           L0_ffn_w_gate, L0_ffn_w_up, L0_ffn_w_down):
    from concourse.bass_utils import run_bass_kernel_spmd

    x = np.ascontiguousarray(np.asarray(L0_attn_x, dtype=np.float32))
    B, N, D = x.shape
    T = (N // 2) // P
    wqkv = np.ascontiguousarray(np.asarray(L0_attn_Wqkv, dtype=np.float32))
    wout = np.ascontiguousarray(np.asarray(L0_attn_Wout, dtype=np.float32))
    bout = np.ascontiguousarray(np.asarray(L0_attn_bout, dtype=np.float32)).reshape(1, D)
    wg = np.ascontiguousarray(np.asarray(L0_ffn_w_gate, dtype=np.float32))
    wu = np.ascontiguousarray(np.asarray(L0_ffn_w_up, dtype=np.float32))
    wd = np.ascontiguousarray(np.asarray(L0_ffn_w_down, dtype=np.float32))

    nc = bass.Bass("TRN2", target_bir_lowering=False)
    build(nc, T, 8, [[0, 1], [2, 3], [4, 5], [6, 7]])

    half = N // 2
    in_maps = []
    for c in range(8):
        b, hf = c // 2, c % 2
        sl = np.ascontiguousarray(x[b, hf * half:(hf + 1) * half, :])
        in_maps.append({"x_in": sl, "wqkv": wqkv, "wout": wout, "bout": bout,
                        "wg": wg, "wu": wu, "wd": wd})

    res = run_bass_kernel_spmd(nc, in_maps, core_ids=list(range(8)))
    out = np.empty((B, N, D), dtype=np.float32)
    for c in range(8):
        b, hf = c // 2, c % 2
        out[b, hf * half:(hf + 1) * half, :] = np.asarray(
            res.results[c]["y_out"], dtype=np.float32)
    return out
